# version 09
# baseline (speedup 1.0000x reference)
import contextlib
import numpy as np
import ml_dtypes
import concourse.bass as bass
import concourse.mybir as mybir
from concourse.bass_utils import run_bass_kernel_spmd
from concourse.bass_isa import ReduceOp as _RO
from concourse.library_config import attn as _LIB_ATTN

F32 = mybir.dt.float32
BF16 = mybir.dt.bfloat16
FP8 = mybir.dt.float8e4
I32 = mybir.dt.int32
AF = mybir.ActivationFunctionType
ALU = mybir.AluOpType
AX = mybir.AxisListType

S = 2048
D = 2048
NT = 16
NK = 16
NKV = 8
HD = 64
NSTRIP = 16
N_CORES = 8
EPS = 1e-5
RECIP_SEED = 0x7EF311C3

T, V, SC, G, SY = "tensor", "vector", "scalar", "gpsimd", "sync"


def _host_consts():
    freqs = (np.float32(1.0) / (np.float32(10000.0) **
             (np.arange(0, HD, 2, dtype=np.float32) / np.float32(HD)))
             ).astype(np.float32)
    angles = np.arange(S, dtype=np.float32)[:, None] * freqs[None, :]
    c32 = np.cos(angles).astype(np.float32).T
    s32 = np.sin(angles).astype(np.float32).T
    C128 = np.concatenate([c32, c32, c32, c32], axis=0)
    S128 = np.concatenate([-s32, s32, -s32, s32], axis=0)
    ident = np.eye(128, dtype=np.float32)
    perm = np.zeros((128, 128), dtype=np.float32)
    for i in range(128):
        perm[i ^ 32, i] = 1.0
    um2 = (-400.0 * (np.arange(128)[None, :] > np.arange(128)[:, None])
           ).astype(np.float32)
    return {
        "C128": C128.astype(ml_dtypes.bfloat16),
        "S128": S128.astype(ml_dtypes.bfloat16),
        "identB": ident.astype(ml_dtypes.bfloat16),
        "PERM": perm.astype(ml_dtypes.bfloat16),
        "UM2": um2.astype(ml_dtypes.bfloat16),
        "id32": ident,
    }


_CONST_DT = {"C128": BF16, "S128": BF16, "identB": BF16, "PERM": BF16,
             "UM2": BF16, "id32": F32}


class Prog:

    def __init__(self):
        self.steps = {T: [], V: [], SC: [], G: [], SY: []}
        self.ctr = {}
        self.sems = {}
        self.mark = {}

    def new_sem(self, name):
        self.ctr[name] = 0

    def op(self, eng, fn, waits=(), inc=None, inc_n=1):
        for w in waits:
            self.wait(eng, w)
        if inc is None:
            self.steps[eng].append(lambda e, fn=fn: (fn(e), None)[1])
            return None
        self.ctr[inc] += inc_n
        tok = (inc, self.ctr[inc])

        def wrapped(e, fn=fn, inc=inc, inc_n=inc_n):
            ins = fn(e)
            ins.then_inc(self.sems[inc], inc_n)
        self.steps[eng].append(wrapped)
        return tok

    def op_dmas(self, eng, fn, waits=(), inc=None):
        for w in waits:
            self.wait(eng, w)

        def wrapped(e, fn=fn, inc=inc):
            for ins in fn(e):
                ins.then_inc(self.sems[inc], 16)
        self.steps[eng].append(wrapped)
        raise RuntimeError("use op_dmas_n")

    def op_dmas_n(self, eng, fn, n_dma, waits=(), inc=None):
        for w in waits:
            self.wait(eng, w)
        self.ctr[inc] += 16 * n_dma
        tok = (inc, self.ctr[inc])

        def wrapped(e, fn=fn, inc=inc):
            ins_list = fn(e)
            assert len(ins_list) == n_dma, (len(ins_list), n_dma)
            for ins in ins_list:
                ins.then_inc(self.sems[inc], 16)
        self.steps[eng].append(wrapped)
        return tok

    def wait(self, eng, token):
        if token is None:
            return
        sem, val = token
        key = (eng, sem)
        if self.mark.get(key, -1) >= val:
            return
        self.mark[key] = val
        self.steps[eng].append(
            lambda e, sem=sem, val=val: e.wait_ge(self.sems[sem], val))

    def install(self, block):
        for eng, attr in ((T, "tensor"), (V, "vector"), (SC, "scalar"),
                          (G, "gpsimd"), (SY, "sync")):
            fns = self.steps[eng]

            def body(e, fns=fns):
                for f in fns:
                    f(e)
            getattr(block, attr)(body)


def build_nc(stop_after=None, debug=False, cc_cores=N_CORES,
             interp_mode=False):
    nc = bass.Bass("TRN2", target_bir_lowering=False)
    consts = _host_consts()

    x_in = nc.dram_tensor("x", [S, D], F32, kind="ExternalInput").ap()
    wq_in = nc.dram_tensor("wq", [D, D], F32, kind="ExternalInput").ap()
    wk_in = nc.dram_tensor("wk", [D, NKV * HD], F32, kind="ExternalInput").ap()
    wv_in = nc.dram_tensor("wv", [D, NKV * HD], F32, kind="ExternalInput").ap()
    wt_in = nc.dram_tensor("wt", [D, D], BF16, kind="ExternalInput").ap()
    y_out = nc.dram_tensor("y", [S, D], F32, kind="ExternalOutput").ap()
    cc_in = nc.dram_tensor("cc_in", [1, 8], F32)
    cc_out = nc.dram_tensor("cc_out", [1, 8], F32)
    const_t = {k: nc.inline_tensor(v, name="h_" + k) for k, v in consts.items()}

    dbg = {}
    if debug:
        for name, shape, dt in (
            ("d_hT", [128, NK * S], BF16), ("d_kc", [128, S], BF16),
            ("d_vt", [128, NT * 520], BF16), ("d_strip", [128, 2 * S], BF16),
            ("d_x2", [128, NT * D], BF16), ("d_stats", [128, 64], F32),
            ("d_nsc", [128, 64], F32), ("d_scl", [128, 16], F32),
            ("d_h2q", [128, NK * S], FP8), ("d_w8", [128, NK * D], FP8),
            ("d_staged", [65, 2 * S], BF16), ("d_ab", [128, 32], F32),
        ):
            dbg[name] = nc.dram_tensor(name, shape, dt,
                                       kind="ExternalOutput").ap()

    p = Prog()
    for s_ in ("T", "V", "Sc", "G",
               "Vf", "Sf",
               "dx0", "dx1", "dq0", "dq1", "dwf0", "dwf1",
               "dm0", "dm1", "dout", "dcc", "dbgS", "gd0", "gd1"):
        p.new_sem(s_)

    stack = contextlib.ExitStack()

    def psm(name, shape):
        return stack.enter_context(nc.psum_tensor(name, shape, F32))

    arena_items = []
    cursor = [0]

    def at(off, name, shape, dt):
        h = nc.virtual_sbuf_tensor(name, shape, dt)
        arena_items.append((h, off))
        return h

    def sbn(name, shape, dt):
        per = 1
        for s_ in shape[1:]:
            per *= s_
        per *= mybir.dt.size(dt)
        assert per % 32 == 0 or per < 32, (name, per)
        off = cursor[0]
        cursor[0] += max(per, 32)
        return at(off, name, shape, dt), off

    x_bf, _ = sbn("x_bf", [128, NT * D], BF16)
    hT, R_HT = sbn("hT", [128, NK * S], BF16)
    w8 = at(R_HT, "w8", [128, NK * D], FP8)
    h2q = at(R_HT + 32768, "h2q", [128, NK * S], FP8)
    vt, R_VT = sbn("vt", [128, NT * 520], BF16)
    ffnb = at(R_VT, "ffnb", [128, 2 * D], BF16)
    id8 = at(R_VT + 8192, "id8", [128, 128], FP8)
    id32v = at(R_VT + 8320, "id32", [128, 128], F32)
    h2qnv = at(R_VT + 8832, "h2qn", [128, 2 * D], FP8)
    xin, R_XIN = sbn("xin", [128, 2 * D], F32)
    pbuf = at(R_XIN, "pbuf", [128, 2 * 1024], BF16)
    stagedA = at(R_XIN + 4096, "stagedA", [65, S], BF16)
    stagedB = at(R_XIN + 8192, "stagedB", [65, S], BF16)
    st1 = at(R_XIN + 12288, "st1", [128, S], BF16)
    outf = at(R_XIN, "outf", [128, 2 * D], F32)
    wv_sb, R_WV = sbn("wv_sb", [128, NK * 512], BF16)
    wf_sb = at(R_WV, "wf_sb", [128, 2 * D], BF16)
    kdup = at(R_WV + 8192, "kdup", [128, 2 * S], BF16)
    h2qn = h2qnv
    kbuf, _ = sbn("kbuf", [128, S], BF16)
    wq_sb, _ = sbn("wq_sb", [128, D], BF16)
    strip, R_STRIP = sbn("strip", [128, 2 * S], BF16)
    xin3 = at(R_STRIP, "xin3", [128, D], F32)
    xin4 = at(R_WV + 8192, "xin4", [128, D], F32)
    straw, _ = sbn("straw", [128, S], BF16)
    st2, _ = sbn("st2", [128, 1024], BF16)
    cs = {}
    for k in consts:
        if k == "id32":
            continue
        cs[k], _ = sbn("c_" + k, list(consts[k].shape), _CONST_DT[k])
    id32 = id32v
    stats, _ = sbn("stats", [128, 64], F32)
    nsc, _ = sbn("nsc", [128, 64], F32)
    ab, _ = sbn("ab", [128, 32], F32)
    bn6, _ = sbn("bn6", [128, 24], F32)
    cpool, _ = sbn("cpool", [128, 8], F32)
    scl, _ = sbn("scl", [128, 16], F32)
    sscr, _ = sbn("sscr", [1, 8], F32)
    dent = stats
    rect = nsc
    rscr = ab

    def commit_arena():
        total = 0
        for h, off in arena_items:
            per = 1
            for s_ in h.shape[1:]:
                per *= s_
            per *= mybir.dt.size(h.dtype)
            total = max(total, off + per)
        if interp_mode:
            fake = 1 << 19
            for h, off in arena_items:
                per = 1
                for s_ in h.shape[1:]:
                    per *= s_
                per *= mybir.dt.size(h.dtype)
                nc._tensor(h.name, h.shape, h.dtype, type="SB")
                m = nc.lookup_mloc(h)
                m.allocated = True
                m.addr = fake
                fake += per + 32
            return
        arena = stack.enter_context(
            nc.sbuf_tensor("arena", [nc.NUM_PARTITIONS, total],
                           mybir.dt.uint8))
        base = nc.lookup_mloc(arena).addr
        for h, off in arena_items:
            nc._tensor(h.name, h.shape, h.dtype, type="SB")
            m = nc.lookup_mloc(h)
            m.allocated = not nc.target_bir_lowering
            m.addr = base + off

    PS_A = psm("PS_A", [128, 1024])
    PS_B = psm("PS_B", [128, 1024])
    PS_C = psm("PS_C", [128, 1024])
    PS_D = psm("PS_D", [128, 1024])

    _barctr = {"Vf": [0], "Sf": [0]}

    def _mkbar(semn):
        ctr = _barctr[semn]

        def bar(e, ins):
            ctr[0] += 1
            ins.then_inc(p.sems[semn], 1)
            e.wait_ge(p.sems[semn], ctr[0])
        return bar
    barV = _mkbar("Vf")
    barS = _mkbar("Sf")

    def fn_consts(e):
        return [e.dma_start(out=cs[k][:], in_=const_t[k].ap())
                for k in const_t if k != "id32"]
    tok_consts = p.op_dmas_n(SY, fn_consts, len(const_t) - 1, inc="dm0")

    def fn_vinit(e):
        e.memset(cpool[:, 0:1], -6.0)
        e.memset(PS_A[:], 0.0)
        e.memset(PS_B[:], 0.0)
        e.memset(PS_C[:], 0.0)
        e.memset(PS_D[:], 0.0)
        return e.memset(bass.AP(vt, 64, [[NT * 520, 128], [520, NT], [65, 8],
                                         [1, 1]]), 1.0)
    tok_vinit = p.op(V, fn_vinit, inc="V")

    def fn_wv(e):
        return e.dma_start(
            out=bass.AP(wv_sb, 0, [[NK * 512, 128], [512, NK], [1, 512]]),
            in_=bass.AP(wv_in.tensor, 0, [[512, 128], [512 * 128, NK],
                                          [1, 512]]))
    tok_wv = p.op(G, fn_wv, inc="dq0", inc_n=16)

    def load_wq(src_ap, col0, par, waits):
        w_ = src_ap.tensor.shape[1]

        def fn(e, col0=col0, w_=w_):
            return e.dma_start(
                out=bass.AP(wq_sb, 0, [[D, 128], [128, NK], [1, 128]]),
                in_=bass.AP(src_ap.tensor, col0,
                            [[w_, 128], [128 * w_, NK], [1, 128]]))
        return p.op(G, fn, waits=waits, inc="dq0" if par == 0 else "dq1",
                    inc_n=16)

    def xsl3(t):
        sl = t % 3
        if sl < 2:
            return xin, sl * D
        return xin3, 0

    def load_x(t, waits):
        tens, c0 = xsl3(t)

        def fn(e, t=t, tens=tens, c0=c0):
            return e.dma_start(out=tens[:, c0:c0 + D],
                               in_=x_in[128 * t:128 * (t + 1), :])
        if t % 2 == 0:
            return p.op(SY, fn, waits=waits, inc="dx0", inc_n=16)
        return p.op(SC, fn, waits=waits, inc="dx1", inc_n=16)

    tok_x = {0: load_x(0, ()), 1: load_x(1, ()), 2: load_x(2, ())}
    tok_hT_ev = {}
    tok_tr_prev = {0: None, 1: None}
    tok_straw_free = {0: None, 1: None}
    tok_stA = {}
    for t in range(NT):
        tens, c0 = xsl3(t)

        def fn_stats(e, t=t, tens=tens, c0=c0):
            for c_ in range(4):
                ins = e.bn_stats(bn6[:, 6 * c_:6 * c_ + 6],
                                 tens[:, c0 + 512 * c_:c0 + 512 * (c_ + 1)])
            barV(e, ins)
            barV(e, e.bn_aggr(stats[:, 48:50], bn6[:, 0:24]))
            e.tensor_copy(stats[:, t:t + 1], stats[:, 48:49])
            return e.tensor_copy(stats[:, 16 + t:17 + t], stats[:, 49:50])
        tok_stA[t] = p.op(V, fn_stats, waits=(tok_x[t], tok_vinit), inc="V")
        if t + 3 < NT:
            tok_x[t + 3] = load_x(t + 3, (tok_stA[t],))

    def fn_batch(e):
        barV(e, e.tensor_scalar(stats[:, 48:64], stats[:, 16:32], EPS, None,
                                ALU.add))
        return e.reciprocal(stats[:, 16:32], stats[:, 48:64])
    tok_bA = p.op(V, fn_batch, waits=(tok_stA[NT - 1],), inc="V")
    tok_bS = p.op(SC, lambda e: e.activation(
        stats[:, 16:32], stats[:, 16:32], AF.Sqrt, bias=0.0, scale=1.0),
        waits=(tok_bA,), inc="Sc")

    def fn_newt(e):
        for _ in range(2):
            barV(e, e.tensor_mul(nsc[:, 0:16], stats[:, 16:32],
                                 stats[:, 16:32]))
            barV(e, e.tensor_mul(nsc[:, 0:16], nsc[:, 0:16],
                                 stats[:, 48:64]))
            barV(e, e.tensor_scalar(nsc[:, 0:16], nsc[:, 0:16], -0.5, 1.5,
                                    ALU.mult, ALU.add))
            barV(e, e.tensor_mul(stats[:, 16:32], stats[:, 16:32],
                                 nsc[:, 0:16]))
        barV(e, e.tensor_mul(stats[:, 32:48], stats[:, 0:16],
                             stats[:, 16:32]))
        return e.tensor_scalar(stats[:, 32:48], stats[:, 32:48], -1.0, None,
                               ALU.mult)
    tok_ln1 = p.op(V, fn_newt, waits=(tok_bS,), inc="V")

    tok_x = {0: load_x(0, (tok_ln1,)), 1: load_x(1, (tok_ln1,)),
             2: load_x(2, (tok_ln1,))}
    for t in range(NT):
        tens, c0 = xsl3(t)
        spar2 = t % 2
        strawX = straw if spar2 == 0 else kbuf
        waits = [tok_x[t], tok_ln1]
        if tok_straw_free[spar2] is not None:
            waits.append(tok_straw_free[spar2])

        def fn_h(e, t=t, tens=tens, c0=c0, strawX=strawX):
            return e.activation(strawX[:, 0:D], tens[:, c0:c0 + D],
                                AF.Identity, bias=stats[:, 32 + t:33 + t],
                                scale=stats[:, 16 + t:17 + t])
        tok_h = p.op(SC, fn_h, waits=waits, inc="Sc")
        tok_st = tok_ln1

        if t + 3 < NT:
            tok_x[t + 3] = load_x(t + 3, (tok_h,))

        for half, psu in ((0, PS_A), (1, PS_B)):
            waits = [tok_h, tok_consts, tok_vinit]
            if tok_tr_prev[half] is not None:
                waits.append(tok_tr_prev[half])

            def fn_tr(e, half=half, psu=psu, strawX=strawX):
                for j8 in range(8):
                    j = half * 8 + j8
                    ins = e.matmul(psu[:, 128 * j8:128 * (j8 + 1)],
                                   strawX[:, 128 * j:128 * (j + 1)],
                                   cs["identB"][:], start=True, stop=True)
                return ins
            tok_tr = p.op(T, fn_tr, waits=waits, inc="T")

            def fn_ev(e, t=t, half=half, psu=psu):
                return e.tensor_copy(
                    bass.AP(hT, S * (half * 8) + 128 * t,
                            [[NK * S, 128], [S, 8], [1, 128]]),
                    psu[:, 0:1024])
            tok_hT_ev[(t, half)] = p.op(V, fn_ev, waits=(tok_tr,), inc="V")
            tok_tr_prev[half] = tok_hT_ev[(t, half)]
        tok_straw_free[spar2] = tok_tr

    tok_hT_done = (tok_hT_ev[(NT - 1, 0)], tok_hT_ev[(NT - 1, 1)])

    from collections import deque
    prodq = deque()
    epiq = deque()

    def drain(q, n):
        for _ in range(min(n, len(q))):
            q.popleft()()

    def drain_all(q):
        drain(q, len(q))

    state = {"psA_free": tok_hT_ev[(NT - 1, 0)],
             "psB_free": tok_hT_ev[(NT - 1, 1)],
             "straw_T": {0: tok_straw_free[0], 1: tok_straw_free[0]},
             "st1_init": None,
             }

    def enq_production(j, out, dst, dst_col0, dst_free_fn):
        out["t1"] = {}
        out["evac"] = {}
        for half in range(2):
            def run_acc(half=half, j=j):
                def fn(e, half=half):
                    for m in range(32):
                        nc2, k = m // 16, m % 16
                        ins = e.matmul(
                            PS_A[:, 512 * nc2:512 * (nc2 + 1)],
                            wq_sb[:, 128 * k:128 * (k + 1)],
                            hT[:, S * k + 1024 * half + 512 * nc2:
                               S * k + 1024 * half + 512 * (nc2 + 1)],
                            start=(k == 0), stop=(k == NK - 1))
                    return ins
                waits = [loads_tok[j], state["psA_free"]] + list(tok_hT_done)
                tok = p.op(T, fn, waits=waits, inc="T")
                out["acc"] = tok
                if half == 1 and j + 1 < len(sched):
                    issue_load(j + 1, (tok,))
            prodq.append(run_acc)

            def run_evac(half=half, out=out):
                def fn(e, half=half):
                    return e.tensor_copy(straw[:, 1024 * half:1024 * (half + 1)],
                                         PS_A[:, 0:1024])
                waits = [out["acc"], state["straw_T"][half]]
                out["evac"][half] = p.op(V, fn, waits=waits, inc="V")
            prodq.append(run_evac)

            def run_swap(half=half, out=out):
                def fn(e, half=half):
                    for nc2 in range(2):
                        ins = e.matmul(PS_A[:, 512 * nc2:512 * (nc2 + 1)],
                                       cs["PERM"][:],
                                       straw[:, 1024 * half + 512 * nc2:
                                             1024 * half + 512 * (nc2 + 1)],
                                       start=True, stop=True)
                    return ins
                tok = p.op(T, fn, waits=[out["evac"][half]], inc="T")
                state["straw_T"][half] = tok
                out["swap"] = tok
            prodq.append(run_swap)

            def run_t1(half=half, out=out):
                def fn(e, half=half):
                    return e.tensor_tensor(
                        st1[:, 1024 * half:1024 * (half + 1)], PS_A[:, 0:1024],
                        cs["S128"][:, 1024 * half:1024 * (half + 1)], ALU.mult)
                waits = [out["swap"]]
                if state["st1_init"] is not None:
                    waits.append(state["st1_init"])
                    state["st1_init"] = None
                tok = p.op(V, fn, waits=waits, inc="V")
                out["t1"][half] = tok
                state["psA_free"] = tok
            prodq.append(run_t1)

            def run_fin(half=half, out=out):
                def fn(e, half=half):
                    barV(e, e.tensor_tensor(
                        st2[:, 0:1024],
                        straw[:, 1024 * half:1024 * (half + 1)],
                        cs["C128"][:, 1024 * half:1024 * (half + 1)],
                        ALU.mult))
                    return e.tensor_add(
                        dst[:, dst_col0 + 1024 * half:
                            dst_col0 + 1024 * (half + 1)],
                        st2[:, 0:1024],
                        st1[:, 1024 * half:1024 * (half + 1)])
                waits = [out["t1"][half]]
                if half == 0:
                    df = dst_free_fn()
                    if df is not None:
                        waits.append(df)
                out["ready"] = p.op(V, fn, waits=waits, inc="V")
            prodq.append(run_fin)

    sched = []
    for kt_ in range(4):
        sched.append((wk_in, 128 * kt_))
        for sp_ in range(4 * kt_, 4 * kt_ + 4):
            sched.append((wq_in, 128 * sp_))
    loads_tok = {}

    def issue_load(j, waits):
        src, col = sched[j]
        loads_tok[j] = load_wq(src, col, j % 2, waits)

    issue_load(0, (tok_wv,))
    kprod = [{} for _ in range(4)]
    qprod = [{} for _ in range(NSTRIP)]
    enq_production(0, kprod[0], kbuf, 0, lambda: tok_straw_free[1])
    enq_production(1, qprod[0], strip, 0, lambda: None)

    tok_v_prev = {0: None, 1: None}
    for t in range(NT):
        hp = t % 2

        def fn_vmm(e, t=t, hp=hp):
            for k in range(NK):
                ins = e.matmul(PS_C[:, 512 * hp:512 * (hp + 1)],
                               hT[:, S * k + 128 * t:S * k + 128 * (t + 1)],
                               wv_sb[:, 512 * k:512 * (k + 1)],
                               start=(k == 0), stop=(k == NK - 1))
            return ins
        waits = [tok_wv] + list(tok_hT_done)
        if tok_v_prev[hp] is not None:
            waits.append(tok_v_prev[hp])
        tok_vmm = p.op(T, fn_vmm, waits=waits, inc="T")

        def fn_vev(e, t=t, hp=hp):
            return e.tensor_copy(
                bass.AP(vt, 520 * t, [[NT * 520, 128], [65, 8], [1, 64]]),
                bass.AP(PS_C, 512 * hp, [[1024, 128], [64, 8], [1, 64]]))
        tok_v_prev[hp] = p.op(V, fn_vev, waits=(tok_vmm, tok_vinit), inc="V")
        tok_wv_free = tok_vmm
        drain(prodq, 2)
    tok_v_done = tok_v_prev[(NT - 1) % 2]
    tok_v_done2 = tok_v_prev[NT % 2]


    def load_wf(t, waits):
        def fn(e, t=t):
            return e.dma_start(out=wf_sb[:, (t % 2) * D:(t % 2 + 1) * D],
                               in_=wt_in[128 * t:128 * (t + 1), :])
        return p.op(SY, fn, waits=waits,
                    inc="dwf0" if t % 2 == 0 else "dwf1", inc_n=16)

    tok_wf = [load_wf(0, (tok_wv_free,)), load_wf(1, (tok_wv_free,))]
    tok_wred = None
    for t in range(NT):
        par = t % 2

        def fn_wred(e, t=t, par=par):
            ins = e.reduce_max(nsc[:, 48 + t:49 + t],
                               wf_sb[:, par * D:(par + 1) * D], AX.X,
                               apply_absolute_value=True)
            if t == NT - 1:
                barV(e, ins)
                ins = e.engine_nop()
            return ins
        tok_wred = p.op(V, fn_wred, waits=(tok_wf[par],), inc="V")
        if t + 2 < NT:
            tok_wf[par] = load_wf(t + 2, (tok_wred,))

    def fn_wamax(e):
        return e.reduce_max(scl[:, 14:15], nsc[:, 48:64], AX.X)
    tok_wa_v = p.op(V, fn_wamax, waits=(tok_wred,), inc="V")

    if stop_after in ("ln1", "kv"):
        _finish(nc, p, stack, dbg, locals(), stop_after)
        return nc

    def fn_kdup(e, g):
        lo = (g % 2) * 64
        dpar = (g % 2) * S
        return [
            e.dma_start(out=kdup[0:64, dpar:dpar + S],
                        in_=kbuf[lo:lo + 64, :]),
            e.dma_start(out=kdup[64:128, dpar:dpar + S],
                        in_=kbuf[lo:lo + 64, :])]

    tok_strip = None
    tok_strip_free = None
    tok_kst = None
    tok_kbuf_free = None
    tok_kdup = None
    tok_scp_free = {0: tok_v_done, 1: None}
    tok_pb_free = {0: None, 1: None}
    tok_psd_free = None
    tok_staged_free = None
    tok_dent_prev = None
    tok_stt = None
    state["st1_init"] = tok_h
    tok_xin_V_free = tok_st
    tok_xin_Sc_free = tok_h
    tok_scp_free[1] = tok_hT_ev[(NT - 1, 1)]
    nlu = {"stt": None, "staged_free": None, "dent_prev": None}
    stt_bank = {0: None, 1: None}

    tok_strip_free_par = [None, None]
    state2 = {"kbuf_free": None}
    kdup_last_read = {0: None, 1: None}
    drain_all(prodq)
    tok_kst = kprod[0]["ready"]
    tok_kdup = p.op_dmas_n(SY, lambda e: fn_kdup(e, 0), 2,
                           waits=(tok_kst, tok_wv_free), inc="dm0")
    tok_strip = qprod[0]["ready"]

    for sp in range(NSTRIP):
        g = sp // 2
        spar = sp % 2
        if sp + 1 < NSTRIP:
            jq = (sp + 1) + (sp + 1) // 4 + 1
            npar = (sp + 1) % 2
            enq_production(jq, qprod[sp + 1], strip, npar * S,
                           lambda npar=npar: tok_strip_free_par[npar])
        if sp % 4 == 2 and sp + 2 < NSTRIP:
            kt = (sp + 2) // 4
            enq_production(5 * kt, kprod[kt], kbuf, 0,
                           lambda: state2["kbuf_free"])

        tok_av = {}
        tok_pvev = None
        last_T_tok = None
        for qc in range(4):
            nsi = 4 * qc + 4
            tok_sc = {}
            for si in range(nsi + 1):
                if qc == 0:
                    drain(epiq, 8 if si == 0 else 2)
                else:
                    drain(epiq, 1)
                drain(prodq, 2 if len(prodq) > 20 else 1)
                if si < nsi:
                    i = si - 4 * qc
                    cst = 128 * i if i > 0 else 0
                    pp = si % 2
                    scp = PS_C if pp == 0 else PS_B

                    def fn_sc(e, si=si, qc=qc, cst=cst, scp=scp,
                              spar=spar, gpar=g % 2, part=(i >= 0)):
                        for base in (0, 64):
                            hh = base // 64
                            ins = e.matmul(
                                scp[:, 512 * hh + cst:512 * (hh + 1)],
                                kdup[base:base + 64,
                                     gpar * S + 128 * si:
                                     gpar * S + 128 * (si + 1)],
                                strip[base:base + 64,
                                      spar * S + 512 * qc + cst:
                                      spar * S + 512 * (qc + 1)],
                                start=True, stop=not part)
                            if part:
                                ins = e.matmul(
                                    scp[:, 512 * hh + cst:
                                        512 * hh + cst + 128],
                                    cs["UM2"][:], cs["identB"][:],
                                    start=False, stop=True)
                        return ins
                    waits = [tok_strip, tok_kdup]
                    if tok_scp_free[pp] is not None:
                        waits.append(tok_scp_free[pp])
                    if qc == 0 and si == 0 and nlu["dent_prev"] is not None:
                        waits.append(nlu["dent_prev"])
                    tok_sc[si] = p.op(T, fn_sc, waits=waits, inc="T")
                    last_T_tok = tok_sc[si]

                    def fn_exp(e, scp=scp, pp=pp):
                        return e.activation(pbuf[:, 1024 * pp:1024 * (pp + 1)],
                                            scp[:, 0:1024], AF.Exp,
                                            bias=cpool[:, 0:1], scale=0.125)
                    waits = [tok_sc[si], tok_vinit, tok_xin_V_free,
                             tok_xin_Sc_free]
                    if tok_pb_free[pp] is not None:
                        waits.append(tok_pb_free[pp])
                    tok_e = p.op(SC, fn_exp, waits=waits, inc="Sc")
                    tok_scp_free[pp] = tok_e

                    tok_sc[si] = (tok_sc[si], (tok_e, None))

                if si > 0:
                    sj = si - 1
                    i = sj - 4 * qc
                    cst = 128 * i if i > 0 else 0
                    pp = sj % 2
                    (_, (tok_e, tok_m)) = tok_sc[sj]

                    def fn_av(e, sj=sj, cst=cst, pp=pp, g=g, nsi=nsi):
                        for hh in range(2):
                            ins = e.matmul(
                                PS_D[0:65, 512 * hh + cst:512 * (hh + 1)],
                                vt[:, 520 * sj + 65 * g:520 * sj + 65 * (g + 1)],
                                pbuf[:, 1024 * pp + 512 * hh + cst:
                                     1024 * pp + 512 * (hh + 1)],
                                start=(sj == 0), stop=(sj == nsi - 1))
                        return ins
                    waits = [tok_e, tok_v_done, tok_v_done2]
                    if tok_m is not None:
                        waits.append(tok_m)
                    if sj == 0 and tok_psd_free is not None:
                        waits.append(tok_psd_free)
                    tok_a = p.op(T, fn_av, waits=waits, inc="T")
                    tok_pb_free[pp] = tok_a
                    last_T_tok = tok_a
                    tok_av[(qc, sj)] = tok_a

            waits = [tok_av[(qc, nsi - 1)]]
            if qc == 0:
                waits += [w for w in (nlu["staged_free"], nlu["dent_prev"],
                                      tok_xin_V_free, tok_xin_Sc_free)
                          if w is not None]

            def fn_pvev(e, qc=qc):
                e.tensor_copy(stagedA[0:65, 512 * qc:512 * (qc + 1)],
                              PS_D[0:65, 0:512])
                return e.tensor_copy(stagedB[0:65, 512 * qc:512 * (qc + 1)],
                                     PS_D[0:65, 512:1024])
            tok_pvev = p.op(V, fn_pvev, waits=waits, inc="V")
            tok_psd_free = tok_pvev

        def mk_epilogue(sp=sp, g=g, spar=spar, tok_pvev=tok_pvev):
            eps = {}

            def run_dtr():
                def fn(e):
                    for hh, stg in ((0, stagedA), (1, stagedB)):
                        for j in range(16):
                            ins = e.matmul(
                                PS_C[:, 512 + 16 * hh + j:
                                     512 + 16 * hh + j + 1],
                                stg[64:65, 128 * j:128 * (j + 1)],
                                cs["identB"][64:65, 64:65],
                                start=True, stop=True)
                    return ins
                waits = [tok_pvev]
                if tok_scp_free[0] is not None:
                    waits.append(tok_scp_free[0])
                eps["dtr"] = p.op(T, fn, waits=waits, inc="T")
            epiq.append(run_dtr)

            def run_rec():
                def fn(e):
                    return e.reciprocal(nsc[:, 0:32], PS_C[:, 512:544])
                eps["rec"] = p.op(V, fn, waits=(eps["dtr"],), inc="V")
                nlu["dent_prev"] = eps["rec"]
            epiq.append(run_rec)

            headA = 4 * g + 2 * spar

            def mk_ctr(a, hj, need_rec=False):
                psu = PS_C if hj == 0 else PS_B
                stg = stagedA if a == 0 else stagedB
                pp = hj

                def run_ctr():
                    def fn(e, a=a, hj=hj, psu=psu, stg=stg):
                        for j8 in range(8):
                            j = hj * 8 + j8
                            ins = e.matmul(
                                psu[:, 512 * a + 64 * j8:
                                    512 * a + 64 * (j8 + 1)],
                                stg[0:64, 128 * j:128 * (j + 1)],
                                cs["identB"][0:64, 0:64],
                                start=True, stop=True)
                        return ins
                    waits = [tok_pvev]
                    if tok_scp_free[pp] is not None:
                        waits.append(tok_scp_free[pp])
                    if need_rec:
                        waits.append(eps["rec"])
                    eps[("ctr", a, hj)] = p.op(T, fn, waits=waits, inc="T")
                    nlu["staged_free"] = eps[("ctr", a, hj)]
                return run_ctr

            def mk_stt(a, hj):
                psu = PS_C if hj == 0 else PS_B
                head = headA + a
                pp = hj

                def run_stt():
                    def fn(e, a=a, hj=hj, psu=psu, head=head):
                        for j8 in range(8):
                            j = hj * 8 + j8
                            xsl = x_bf[:, D * j + 64 * head:
                                       D * j + 64 * (head + 1)]
                            ins = e.tensor_scalar(
                                xsl,
                                psu[:, 512 * a + 64 * j8:
                                    512 * a + 64 * (j8 + 1)],
                                nsc[:, 16 * a + j:16 * a + j + 1],
                                None, ALU.mult)
                        return ins
                    tok = p.op(V, fn, waits=(eps[("ctr", a, hj)],
                                             eps["rec"]), inc="V")
                    nlu["stt"] = tok
                    tok_scp_free[pp] = tok
                return run_stt

            epiq.append(mk_ctr(0, 0))
            epiq.append(mk_ctr(0, 1))
            epiq.append(mk_ctr(1, 0, need_rec=True))
            epiq.append(mk_ctr(1, 1))
            epiq.append(mk_stt(0, 0))
            epiq.append(mk_stt(1, 0))
            epiq.append(mk_stt(0, 1))
            epiq.append(mk_stt(1, 1))
        mk_epilogue()

        tok_strip_free = last_T_tok
        tok_strip_free_par[spar] = last_T_tok
        drain_all(prodq)
        if sp + 1 < NSTRIP:
            ng = (sp + 1) // 2
            if (sp + 1) % 4 == 0:
                tok_kst = kprod[(sp + 1) // 4]["ready"]
            if (sp + 1) % 2 == 0:
                tok_kdup = p.op_dmas_n(
                    SY, lambda e, ng=ng: fn_kdup(e, ng), 2,
                    waits=(tok_kst, kdup_last_read.get(ng % 2)), inc="dm0")
                if ng % 2 == 1:
                    state2["kbuf_free"] = tok_kdup
            tok_strip = qprod[sp + 1]["ready"]
        if spar == 1:
            kdup_last_read[g % 2] = last_T_tok
    drain_all(epiq)
    drain_all(prodq)
    tok_stt = nlu["stt"]
    tok_staged_free = nlu["staged_free"]

    if stop_after in ("strip0", "attn"):
        _finish(nc, p, stack, dbg, locals(), stop_after)
        return nc

    tok_x = {0: load_x(0, (tok_stt, tok_staged_free, tok_strip_free)),
             1: load_x(1, (tok_stt, tok_staged_free, tok_strip_free)),
             2: load_x(2, (tok_stt, tok_staged_free, tok_strip_free))}
    tok_s2 = {}
    for t in range(NT):
        tens, c0 = xsl3(t)

        def fn_x2(e, t=t, tens=tens, c0=c0):
            xsl = tens[:, c0:c0 + D]
            return e.tensor_add(xsl, xsl, x_bf[:, t * D:(t + 1) * D])
        tok_x2 = p.op(V, fn_x2, waits=(tok_x[t], tok_stt), inc="V")

        def fn_s2(e, t=t, tens=tens, c0=c0):
            xv_ = tens[:, c0:c0 + D]
            for c_ in range(4):
                ins = e.bn_stats(bn6[:, 6 * c_:6 * c_ + 6],
                                 tens[:, c0 + 512 * c_:c0 + 512 * (c_ + 1)])
            barV(e, ins)
            barV(e, e.bn_aggr(stats[:, 48:50], bn6[:, 0:24]))
            e.tensor_copy(stats[:, t:t + 1], stats[:, 48:49])
            e.tensor_copy(stats[:, 16 + t:17 + t], stats[:, 49:50])
            e.reduce_max(nsc[:, 16 + t:17 + t], xv_, AX.X)
            return e.tensor_reduce(nsc[:, 32 + t:33 + t], xv_, AX.X, ALU.min)
        tok_s2[t] = p.op(V, fn_s2, waits=(tok_x2,), inc="V")
        if t + 3 < NT:
            tok_x[t + 3] = load_x(t + 3, (tok_s2[t],))

    def fn_b2(e):
        barV(e, e.tensor_scalar(stats[:, 48:64], stats[:, 16:32], EPS, None,
                                ALU.add))
        return e.reciprocal(stats[:, 16:32], stats[:, 48:64])
    tok_b2 = p.op(V, fn_b2, waits=(tok_s2[NT - 1],), inc="V")
    tok_b2s = p.op(SC, lambda e: e.activation(
        stats[:, 16:32], stats[:, 16:32], AF.Sqrt, bias=0.0, scale=1.0),
        waits=(tok_b2,), inc="Sc")

    def fn_n2(e):
        for _ in range(2):
            barV(e, e.tensor_mul(nsc[:, 0:16], stats[:, 16:32],
                                 stats[:, 16:32]))
            barV(e, e.tensor_mul(nsc[:, 0:16], nsc[:, 0:16],
                                 stats[:, 48:64]))
            barV(e, e.tensor_scalar(nsc[:, 0:16], nsc[:, 0:16], -0.5, 1.5,
                                    ALU.mult, ALU.add))
            barV(e, e.tensor_mul(stats[:, 16:32], stats[:, 16:32],
                                 nsc[:, 0:16]))
        barV(e, e.tensor_mul(stats[:, 32:48], stats[:, 0:16],
                             stats[:, 16:32]))
        barV(e, e.tensor_scalar(stats[:, 32:48], stats[:, 32:48], -1.0, None,
                                ALU.mult))
        barV(e, e.tensor_sub(nsc[:, 16:32], nsc[:, 16:32], stats[:, 0:16]))
        barV(e, e.tensor_sub(nsc[:, 32:48], stats[:, 0:16], nsc[:, 32:48]))
        barV(e, e.tensor_max(nsc[:, 16:32], nsc[:, 16:32], nsc[:, 32:48]))
        barV(e, e.tensor_mul(nsc[:, 0:16], nsc[:, 16:32], stats[:, 16:32]))
        return e.engine_nop()
    tok_ln2 = p.op(V, fn_n2, waits=(tok_b2s,), inc="V")

    tok_id32 = p.op(SY, lambda e: e.dma_start(
        out=id32[:], in_=const_t["id32"].ap()),
        waits=(tok_strip_free,), inc="dm1", inc_n=16)

    def fn_wtr(e):
        return e.matmul(PS_C[0:1, 0:128], scl[:, 14:15], id32[:],
                        start=True, stop=True)
    waits = [tok_wa_v, tok_id32, tok_scp_free[0], tok_stt]
    tok_wtr = p.op(T, fn_wtr, waits=waits, inc="T")

    def fn_wscale(e):
        barV(e, e.reduce_max(nsc[0:1, 33:34], PS_C[0:1, 0:128], AX.X))
        barV(e, e.tensor_scalar(scl[0:1, 10:11], nsc[0:1, 33:34],
                                float(np.float32(1.0) / np.float32(448.0)),
                                1e-12, ALU.mult, ALU.max))
        barV(e, e.tensor_scalar(sscr[0:1, 0:1], scl[0:1, 10:11], 2.0, None,
                                ALU.mult))
        return e.reciprocal(sscr[0:1, 1:2], sscr[0:1, 0:1])
    tok_wsc = p.op(V, fn_wscale, waits=(tok_wtr,), inc="V")

    tok_qsw = p.op(SY, lambda e: e.dma_start(
        out=scl[:, 12:13], in_=bass.AP(sscr, 1, [[8, 1], [0, 128]])),
        waits=(tok_wsc,), inc="dm0", inc_n=16)

    tok_am_v = p.op(V, lambda e: e.reduce_max(nsc[:, 16:17], nsc[:, 0:16],
                                              AX.X), waits=(tok_ln2,),
                    inc="V")

    def fn_atr(e):
        return e.matmul(PS_C[0:1, 128:256], nsc[:, 16:17], id32[:],
                        start=True, stop=True)
    tok_atr = p.op(T, fn_atr, waits=(tok_am_v, tok_id32), inc="T")

    def fn_amr(e):
        barV(e, e.reduce_max(nsc[0:1, 17:18], PS_C[0:1, 128:256], AX.X))
        return e.tensor_copy(scl[0:1, 0:8],
                             bass.AP(nsc, 17, [[64, 1], [0, 8]]))
    tok_amr = p.op(V, fn_amr, waits=(tok_atr,), inc="V")

    tok_cc1 = p.op(G, lambda e: e.dma_start(out=cc_in.ap(),
                                            in_=scl[0:1, 0:8]),
                   waits=(tok_amr,), inc="dq0", inc_n=16)
    tok_cc2 = p.op(G, lambda e: e.collective_compute(
        "AllReduce", ALU.max, replica_groups=[list(range(cc_cores))],
        ins=[cc_in.ap()], outs=[cc_out.ap()]), waits=(tok_cc1,), inc="dcc")
    tok_cc3 = p.op(G, lambda e: e.dma_start(out=scl[0:1, 8:9],
                                            in_=cc_out.ap()[0:1, 0:1]),
                   waits=(tok_cc2,), inc="dq1", inc_n=16)

    def fn_xscale(e):
        barV(e, e.tensor_scalar(scl[0:1, 9:10], scl[0:1, 8:9],
                                float(np.float32(1.0) / np.float32(448.0)),
                                1e-12, ALU.mult, ALU.max))
        barV(e, e.tensor_scalar(sscr[0:1, 2:3], scl[0:1, 9:10], 2.0, None,
                                ALU.mult))
        e.reciprocal(sscr[0:1, 3:4], sscr[0:1, 2:3])
        barV(e, e.tensor_mul(sscr[0:1, 4:5], scl[0:1, 9:10],
                             scl[0:1, 10:11]))
        barV(e, e.tensor_scalar(sscr[0:1, 5:6], sscr[0:1, 4:5], 4.0, None,
                                ALU.mult))
        return e.engine_nop()
    tok_xsc = p.op(V, fn_xscale, waits=(tok_cc3, tok_wsc), inc="V")

    tok_qsx = p.op(SY, lambda e: e.dma_start(
        out=scl[:, 11:12], in_=bass.AP(sscr, 3, [[8, 1], [0, 128]])),
        waits=(tok_xsc,), inc="dm0", inc_n=16)
    tok_c4 = p.op(SY, lambda e: e.dma_start(
        out=scl[:, 13:14], in_=bass.AP(sscr, 5, [[8, 1], [0, 128]])),
        waits=(tok_xsc,), inc="dm1", inc_n=16)

    def fn_ab(e):
        e.tensor_tensor(ab[:, 0:16], stats[:, 16:32],
                        bass.AP(scl, 11, [[16, 128], [0, 16]]), ALU.mult)
        return e.tensor_tensor(ab[:, 16:32], stats[:, 32:48],
                               bass.AP(scl, 11, [[16, 128], [0, 16]]),
                               ALU.mult)
    tok_ab = p.op(V, fn_ab, waits=(tok_qsx, tok_ln2), inc="V")

    tok_wf2 = [load_wf(0, (tok_wred,)), load_wf(1, ())]
    tok_w8 = None
    for t in range(NT):
        par = t % 2

        def fn_w8(e, t=t, par=par):
            return e.activation(w8[:, t * D:(t + 1) * D],
                                wf_sb[:, par * D:(par + 1) * D], AF.Copy,
                                bias=0.0, scale=scl[:, 12:13])
        tok_w8 = p.op(SC, fn_w8, waits=(tok_wf2[par], tok_qsw), inc="Sc")
        if t + 2 < NT:
            tok_wf2[par] = load_wf(t + 2, (tok_w8,))
    tok_w8_done = tok_w8

    def fn_id8(e):
        return e.activation(id8[:], cs["identB"][:], AF.Copy, bias=0.0,
                            scale=1.0)
    tok_id8 = p.op(SC, fn_id8, waits=(tok_consts, tok_strip_free), inc="Sc")

    xslots = [(xin, 0), (xin, D), (xin3, 0), (xin4, 0)]

    def xsl_t(t):
        return xslots[t % 4]

    def load_x5(t, waits):
        tens, c0 = xsl_t(t)

        def fn(e, t=t, tens=tens, c0=c0):
            return e.dma_start(out=tens[:, c0:c0 + D],
                               in_=x_in[128 * t:128 * (t + 1), :])
        return p.op(G, fn, waits=tuple(w for w in waits if w is not None),
                    inc="dq0" if t % 2 == 0 else "dq1", inc_n=16)

    psbanks = [(PS_C, 0), (PS_C, 512), (PS_D, 0), (PS_D, 512)]
    tok_h2ev = {}
    tok_h2tr = {0: None, 1: None}
    tok_q = [None, None]
    tok_qd = {}
    tok_x5 = {}
    tok_fev = {}
    tok_addv = {0: None, 1: None}
    tok_outd = {}

    tok_x5[0] = load_x5(0, (tok_ln2,))
    tok_x5[1] = load_x5(1, (tok_ln2,))
    tok_x5[2] = load_x5(2, (tok_ln2,))
    tok_x5[3] = load_x5(3, (tok_strip_free, kdup_last_read.get(0),
                            kdup_last_read.get(1)))

    def emit_qx(t):
        par = t % 2
        tens, c0 = xsl_t(t)

        def fn_x2(e, t=t, tens=tens, c0=c0):
            xsl = tens[:, c0:c0 + D]
            return e.tensor_add(xsl, xsl, x_bf[:, t * D:(t + 1) * D])
        tok_x2q = p.op(V, fn_x2, waits=(tok_x5[t],), inc="V")

        def fn_q(e, t=t, par=par, tens=tens, c0=c0):
            return e.activation(h2qn[:, par * D:(par + 1) * D],
                                tens[:, c0:c0 + D], AF.Identity,
                                bias=ab[:, 16 + t:17 + t],
                                scale=ab[:, t:t + 1])
        waits = [tok_ab, tok_x2q]
        if t < 2:
            waits.append(tok_strip_free)
        if tok_h2tr[par] is not None:
            waits.append(tok_h2tr[par])
        tok_q[par] = tok_qd[t] = p.op(SC, fn_q, waits=waits, inc="Sc")

    def emit_tr(t):
        par = t % 2
        tok_h2ev[t] = {}
        for half, psu in ((0, PS_A), (1, PS_B)):
            def fn_qtr(e, par=par, half=half, psu=psu):
                for j8 in range(8):
                    j = half * 8 + j8
                    ins = e.matmul(
                        psu[:, 128 * j8:128 * (j8 + 1)],
                        h2qn[:, par * D + 128 * j:par * D + 128 * (j + 1)],
                        id8[:], start=True, stop=True)
                return ins
            waits = [tok_q[par], tok_id8]
            if t >= 1:
                waits += list(tok_h2ev[t - 1][half])
            if t == 0 and half == 0:
                waits.append(state["psA_free"])
                waits.append(tok_scp_free[1])
            tok_qt = p.op(T, fn_qtr, waits=waits, inc="T")
            if half == 1:
                tok_h2tr[par] = tok_qt

            if half == 0:
                def fn_evA(e, t=t, psu=psu):
                    return e.tensor_copy(
                        bass.AP(h2q, 128 * t,
                                [[NK * S, 128], [S, 8], [1, 128]]),
                        psu[:, 0:1024])
                tok_h2ev[t][0] = (p.op(V, fn_evA, waits=(tok_qt,), inc="V"),)
            else:
                def fn_evB1(e, t=t, psu=psu):
                    return e.tensor_copy(
                        bass.AP(h2q, S * 8 + 128 * t,
                                [[NK * S, 128], [S, 3], [1, 128]]),
                        psu[:, 0:384])
                evB1 = p.op(V, fn_evB1, waits=(tok_qt,), inc="V")

                def fn_evB2(e, t=t, psu=psu):
                    return e.activation(
                        bass.AP(h2q, S * 11 + 128 * t,
                                [[NK * S, 128], [S, 5], [1, 128]]),
                        psu[:, 384:1024], AF.Copy, bias=0.0, scale=1.0)
                evB2 = p.op(SC, fn_evB2, waits=(tok_qt,), inc="Sc")
                tok_h2ev[t][1] = (evB1, evB2)

    def emit_ffn(t):
        par = t % 2
        tens, c0 = xsl_t(t)
        for oc in range(4):
            psu, off = psbanks[oc]

            def fn_fmm(e, t=t, oc=oc, psu=psu, off=off):
                for kk in range(NK // 2):
                    ins = e.matmul(
                        psu[:, off:off + 512],
                        bass.AP(h2q, S * 2 * kk + 128 * t,
                                [[NK * S, 128], [S, 2], [1, 128]]),
                        bass.AP(w8, D * 2 * kk + 512 * oc,
                                [[NK * D, 128], [D, 2], [1, 512]]),
                        start=(kk == 0), stop=(kk == NK // 2 - 1),
                        perf_mode=mybir.MatmulPerfMode.DoubleRow)
                return ins
            waits = [tok_h2ev[t][0][0], tok_h2ev[t][1][0], tok_h2ev[t][1][1],
                     tok_w8_done]
            if t == 0:
                waits += [tok_scp_free[0], tok_psd_free]
            if tok_fev.get((t - 1, oc)) is not None:
                waits.append(tok_fev[(t - 1, oc)])
            tok_fmm = p.op(T, fn_fmm, waits=waits, inc="T")

            def fn_fev(e, t=t, oc=oc, par=par, psu=psu, off=off):
                return e.activation(
                    ffnb[:, par * D + 512 * oc:par * D + 512 * (oc + 1)],
                    psu[:, off:off + 512], AF.Copy, bias=0.0,
                    scale=scl[:, 13:14])
            waits = [tok_fmm, tok_c4]
            if tok_addv[par] is not None and oc == 0:
                waits.append(tok_addv[par])
            tok_fev[(t, oc)] = p.op(SC, fn_fev, waits=waits, inc="Sc")

        def fn_add(e, t=t, par=par, tens=tens, c0=c0):
            xsl = tens[:, c0:c0 + D]
            return e.tensor_add(xsl, xsl, ffnb[:, par * D:(par + 1) * D])
        waits = [tok_fev[(t, oc_)] for oc_ in range(4)]
        tok_addv[par] = p.op(V, fn_add, waits=waits, inc="V")

        def fn_out(e, t=t, tens=tens, c0=c0):
            return e.dma_start(out=y_out[128 * t:128 * (t + 1), :],
                               in_=tens[:, c0:c0 + D])
        tok_outd[t] = p.op(SY, fn_out, waits=(tok_addv[par],), inc="dout",
                           inc_n=16)
        if t + 4 < NT:
            tok_x5[t + 4] = load_x5(t + 4, (tok_outd[t], tok_qd[t]))

    emit_qx(0)
    for t in range(NT):
        if t + 1 < NT:
            emit_qx(t + 1)
        emit_tr(t)
        if t >= 1:
            emit_ffn(t - 1)
    emit_ffn(NT - 1)

    p.wait(SY, ("dout", 16 * NT))

    if stop_after == "quant":
        _finish(nc, p, stack, dbg, locals(), stop_after)
        return nc

    _finish(nc, p, stack, dbg, locals(), None)
    return nc


_STAGE_DBG = {
    "ln1": ("d_hT", "d_x2", "d_stats"),
    "kv": ("d_hT", "d_vt", "d_x2", "d_stats", "d_nsc", "d_scl"),
    "attn": ("d_hT", "d_vt", "d_x2", "d_stats", "d_nsc", "d_scl", "d_kc",
             "d_strip", "d_staged"),
    "quant": ("d_hT", "d_vt", "d_x2", "d_stats", "d_nsc", "d_scl", "d_kc",
              "d_strip", "d_staged", "d_h2q", "d_w8", "d_ab"),
    None: ("d_hT", "d_vt", "d_x2", "d_stats", "d_nsc", "d_scl", "d_kc",
           "d_strip", "d_staged", "d_h2q", "d_w8", "d_ab"),
}


def _finish(nc, p, stack, dbg, env, stop_after):
    env["commit_arena"]()
    dbg = {k: v for k, v in dbg.items() if k in _STAGE_DBG[stop_after]}
    if dbg:
        srcmap = {"d_hT": "hT", "d_kc": "kbuf", "d_vt": "vt", "d_strip": "strip",
                  "d_x2": "x_bf", "d_stats": "stats", "d_nsc": "nsc",
                  "d_scl": "scl", "d_h2q": "h2q", "d_w8": "w8",
                  "d_ab": None, "d_staged": None}
        for eng in (SY,):
            for semn in ("T", "V", "Sc", "G"):
                p.wait(eng, (semn, p.ctr[semn]))

        def fn_dbg(e):
            out = []
            for name, ap_ in dbg.items():
                if name == "d_staged":
                    out.append(e.dma_start(out=ap_[0:65, 0:S],
                                           in_=env["stagedA"][:]))
                    out.append(e.dma_start(out=ap_[0:65, S:2 * S],
                                           in_=env["stagedB"][:]))
                elif name == "d_ab":
                    out.append(e.dma_start(out=ap_[:, 0:32],
                                           in_=env["ab"][:]))
                else:
                    out.append(e.dma_start(out=ap_, in_=env[srcmap[name]][:]))
            return out
        n_dma = len(dbg) + (1 if "d_staged" in dbg else 0)
        tok_dbg = p.op_dmas_n(SY, fn_dbg, n_dma, inc="dbgS")
        p.wait(SY, tok_dbg)

    with nc.Block() as block, contextlib.ExitStack() as semstack:
        for name in p.ctr:
            p.sems[name] = semstack.enter_context(nc.semaphore(name))
        p.install(block)
    stack.close()


def _prep_inputs(L0_attn_x, L0_attn_Wq, L0_attn_Wk, L0_attn_Wv,
                 L0_ffn_weight):
    qperm = np.concatenate(
        [h * HD + np.concatenate([np.arange(0, HD, 2), np.arange(1, HD, 2)])
         for h in range(32)])
    kperm = qperm[:NKV * HD]
    wq = np.ascontiguousarray(np.asarray(L0_attn_Wq)[:, qperm])
    wk = np.ascontiguousarray(np.asarray(L0_attn_Wk)[:, kperm])
    wv = np.ascontiguousarray(np.asarray(L0_attn_Wv))
    wt = np.ascontiguousarray(np.asarray(L0_ffn_weight).T)
    xs = [np.ascontiguousarray(np.asarray(L0_attn_x)[b])
          for b in range(N_CORES)]
    return xs, wq, wk, wv, wt


def kernel(L0_attn_x, L0_attn_Wq, L0_attn_Wk, L0_attn_Wv, L0_ffn_weight):
    xs, wq, wk, wv, wt = _prep_inputs(L0_attn_x, L0_attn_Wq, L0_attn_Wk,
                                      L0_attn_Wv, L0_ffn_weight)
    nc = build_nc()
    in_maps = [{"x": xs[b], "wq": wq, "wk": wk, "wv": wv, "wt": wt}
               for b in range(N_CORES)]
    res = run_bass_kernel_spmd(nc, in_maps, core_ids=list(range(N_CORES)))
    out = np.stack([np.asarray(res.results[b]["y"]) for b in range(N_CORES)],
                   axis=0)
    return out.astype(np.float32)


# version 10 (final)
# speedup vs baseline: 1.1536x; 1.1536x over previous
import contextlib
import numpy as np
import ml_dtypes
import concourse.bass as bass
import concourse.mybir as mybir
from concourse.bass_utils import run_bass_kernel_spmd
from concourse.bass_isa import ReduceOp as _RO
from concourse.library_config import attn as _LIB_ATTN

F32 = mybir.dt.float32
BF16 = mybir.dt.bfloat16
FP8 = mybir.dt.float8e4
I32 = mybir.dt.int32
AF = mybir.ActivationFunctionType
ALU = mybir.AluOpType
AX = mybir.AxisListType

S = 2048
D = 2048
NT = 16
NK = 16
NKV = 8
HD = 64
NSTRIP = 16
N_CORES = 8
EPS = 1e-5
RECIP_SEED = 0x7EF311C3

T, V, SC, G, SY = "tensor", "vector", "scalar", "gpsimd", "sync"


def _host_consts():
    freqs = (np.float32(1.0) / (np.float32(10000.0) **
             (np.arange(0, HD, 2, dtype=np.float32) / np.float32(HD)))
             ).astype(np.float32)
    angles = np.arange(S, dtype=np.float32)[:, None] * freqs[None, :]
    c32 = np.cos(angles).astype(np.float32).T
    s32 = np.sin(angles).astype(np.float32).T
    C128 = np.concatenate([c32, c32, c32, c32], axis=0)
    S128 = np.concatenate([-s32, s32, -s32, s32], axis=0)
    ident = np.eye(128, dtype=np.float32)
    perm = np.zeros((128, 128), dtype=np.float32)
    for i in range(128):
        perm[i ^ 32, i] = 1.0
    um2 = (-400.0 * (np.arange(128)[None, :] > np.arange(128)[:, None])
           ).astype(np.float32)
    return {
        "C128": C128.astype(ml_dtypes.bfloat16),
        "S128": S128.astype(ml_dtypes.bfloat16),
        "identB": ident.astype(ml_dtypes.bfloat16),
        "PERM": perm.astype(ml_dtypes.bfloat16),
        "UM2": um2.astype(ml_dtypes.bfloat16),
        "id32": ident,
    }


_CONST_DT = {"C128": BF16, "S128": BF16, "identB": BF16, "PERM": BF16,
             "UM2": BF16, "id32": F32}


class Prog:

    def __init__(self):
        self.steps = {T: [], V: [], SC: [], G: [], SY: []}
        self.ctr = {}
        self.sems = {}
        self.mark = {}

    def new_sem(self, name):
        self.ctr[name] = 0

    def op(self, eng, fn, waits=(), inc=None, inc_n=1):
        for w in waits:
            self.wait(eng, w)
        if inc is None:
            self.steps[eng].append(lambda e, fn=fn: (fn(e), None)[1])
            return None
        self.ctr[inc] += inc_n
        tok = (inc, self.ctr[inc])

        def wrapped(e, fn=fn, inc=inc, inc_n=inc_n):
            ins = fn(e)
            ins.then_inc(self.sems[inc], inc_n)
        self.steps[eng].append(wrapped)
        return tok

    def op_dmas(self, eng, fn, waits=(), inc=None):
        for w in waits:
            self.wait(eng, w)

        def wrapped(e, fn=fn, inc=inc):
            for ins in fn(e):
                ins.then_inc(self.sems[inc], 16)
        self.steps[eng].append(wrapped)
        raise RuntimeError("use op_dmas_n")

    def op_dmas_n(self, eng, fn, n_dma, waits=(), inc=None):
        for w in waits:
            self.wait(eng, w)
        self.ctr[inc] += 16 * n_dma
        tok = (inc, self.ctr[inc])

        def wrapped(e, fn=fn, inc=inc):
            ins_list = fn(e)
            assert len(ins_list) == n_dma, (len(ins_list), n_dma)
            for ins in ins_list:
                ins.then_inc(self.sems[inc], 16)
        self.steps[eng].append(wrapped)
        return tok

    def wait(self, eng, token):
        if token is None:
            return
        sem, val = token
        key = (eng, sem)
        if self.mark.get(key, -1) >= val:
            return
        self.mark[key] = val
        self.steps[eng].append(
            lambda e, sem=sem, val=val: e.wait_ge(self.sems[sem], val))

    def install(self, block):
        for eng, attr in ((T, "tensor"), (V, "vector"), (SC, "scalar"),
                          (G, "gpsimd"), (SY, "sync")):
            fns = self.steps[eng]

            def body(e, fns=fns):
                for f in fns:
                    f(e)
            getattr(block, attr)(body)


def build_nc(stop_after=None, debug=False, cc_cores=N_CORES,
             interp_mode=False):
    nc = bass.Bass("TRN2", target_bir_lowering=False)
    consts = _host_consts()

    x_in = nc.dram_tensor("x", [S, D], F32, kind="ExternalInput").ap()
    wq_in = nc.dram_tensor("wq", [D, D], F32, kind="ExternalInput").ap()
    wk_in = nc.dram_tensor("wk", [D, NKV * HD], F32, kind="ExternalInput").ap()
    wv_in = nc.dram_tensor("wv", [D, NKV * HD], F32, kind="ExternalInput").ap()
    wt_in = nc.dram_tensor("wt", [D, D], BF16, kind="ExternalInput").ap()
    y_out = nc.dram_tensor("y", [S, D], F32, kind="ExternalOutput").ap()
    cc_in = nc.dram_tensor("cc_in", [1, 8], F32)
    cc_out = nc.dram_tensor("cc_out", [1, 8], F32)
    const_t = {k: nc.inline_tensor(v, name="h_" + k) for k, v in consts.items()}

    dbg = {}
    if debug:
        for name, shape, dt in (
            ("d_hT", [128, NK * S], BF16), ("d_kc", [128, S], BF16),
            ("d_vt", [128, NT * 520], BF16), ("d_strip", [128, 2 * S], BF16),
            ("d_x2", [128, NT * D], BF16), ("d_stats", [128, 64], F32),
            ("d_nsc", [128, 64], F32), ("d_scl", [128, 16], F32),
            ("d_h2q", [128, NK * S], FP8), ("d_w8", [128, NK * D], FP8),
            ("d_staged", [65, 2 * S], BF16), ("d_ab", [128, 32], F32),
        ):
            dbg[name] = nc.dram_tensor(name, shape, dt,
                                       kind="ExternalOutput").ap()

    p = Prog()
    for s_ in ("T", "V", "Sc", "G",
               "Vf", "Sf",
               "dx0", "dx1", "dx2", "dq0", "dq1", "dwf0", "dwf1",
               "dm0", "dm1", "dout", "dcc", "dbgS", "gd0", "gd1"):
        p.new_sem(s_)

    stack = contextlib.ExitStack()

    def psm(name, shape):
        return stack.enter_context(nc.psum_tensor(name, shape, F32))

    arena_items = []
    cursor = [0]

    def at(off, name, shape, dt):
        h = nc.virtual_sbuf_tensor(name, shape, dt)
        arena_items.append((h, off))
        return h

    def sbn(name, shape, dt):
        per = 1
        for s_ in shape[1:]:
            per *= s_
        per *= mybir.dt.size(dt)
        assert per % 32 == 0 or per < 32, (name, per)
        off = cursor[0]
        cursor[0] += max(per, 32)
        return at(off, name, shape, dt), off

    x_bf, _ = sbn("x_bf", [128, NT * D], BF16)
    hT, R_HT = sbn("hT", [128, NK * S], BF16)
    w8 = at(R_HT, "w8", [128, NK * D], FP8)
    h2q = at(R_HT + 32768, "h2q", [128, NK * S], FP8)
    vt, R_VT = sbn("vt", [128, NT * 520], BF16)
    ffnb = at(R_VT, "ffnb", [128, 2 * D], BF16)
    id8 = at(R_VT + 8192, "id8", [128, 128], FP8)
    id32v = at(R_VT + 8320, "id32", [128, 128], F32)
    h2qnv = at(R_VT + 8832, "h2qn", [128, 2 * D], FP8)
    xin, R_XIN = sbn("xin", [128, 2 * D], F32)
    pbuf = at(R_XIN, "pbuf", [128, 2 * 1024], BF16)
    stagedA = at(R_XIN + 4096, "stagedA", [65, S], BF16)
    stagedB = at(R_XIN + 8192, "stagedB", [65, S], BF16)
    st1 = at(R_XIN + 12288, "st1", [128, S], BF16)
    outf = at(R_XIN, "outf", [128, 2 * D], F32)
    wv_sb, R_WV = sbn("wv_sb", [128, NK * 512], BF16)
    wf_sb = at(R_WV, "wf_sb", [128, 2 * D], BF16)
    kdup = at(R_WV + 8192, "kdup", [128, 2 * S], BF16)
    h2qn = h2qnv
    kbuf, _ = sbn("kbuf", [128, S], BF16)
    wq_sb, _ = sbn("wq_sb", [128, D], BF16)
    strip, R_STRIP = sbn("strip", [128, 2 * S], BF16)
    xin3 = at(R_STRIP, "xin3", [128, D], F32)
    xin4 = at(R_WV + 8192, "xin4", [128, D], F32)
    straw, R_STRAW = sbn("straw", [128, S], BF16)
    wrowA = at(R_STRAW, "wrowA", [1, 128], F32)
    st2, _ = sbn("st2", [128, 1024], BF16)
    cs = {}
    for k in consts:
        if k == "id32":
            continue
        cs[k], _ = sbn("c_" + k, list(consts[k].shape), _CONST_DT[k])
    id32 = id32v
    stats, R_STATS = sbn("stats", [128, 64], F32)
    wrowW = at(R_STATS, "wrowW", [1, 128], F32)
    nsc, _ = sbn("nsc", [128, 64], F32)
    ab, _ = sbn("ab", [128, 32], F32)
    bn6, _ = sbn("bn6", [128, 24], F32)
    cpool, _ = sbn("cpool", [128, 8], F32)
    scl, _ = sbn("scl", [128, 16], F32)
    sscr, _ = sbn("sscr", [1, 8], F32)
    dent = stats
    rect = nsc
    rscr = ab

    def commit_arena():
        total = 0
        for h, off in arena_items:
            per = 1
            for s_ in h.shape[1:]:
                per *= s_
            per *= mybir.dt.size(h.dtype)
            total = max(total, off + per)
        if interp_mode:
            fake = 1 << 19
            for h, off in arena_items:
                per = 1
                for s_ in h.shape[1:]:
                    per *= s_
                per *= mybir.dt.size(h.dtype)
                nc._tensor(h.name, h.shape, h.dtype, type="SB")
                m = nc.lookup_mloc(h)
                m.allocated = True
                m.addr = fake
                fake += per + 32
            return
        arena = stack.enter_context(
            nc.sbuf_tensor("arena", [nc.NUM_PARTITIONS, total],
                           mybir.dt.uint8))
        base = nc.lookup_mloc(arena).addr
        for h, off in arena_items:
            nc._tensor(h.name, h.shape, h.dtype, type="SB")
            m = nc.lookup_mloc(h)
            m.allocated = not nc.target_bir_lowering
            m.addr = base + off

    PS_A = psm("PS_A", [128, 1024])
    PS_B = psm("PS_B", [128, 1024])
    PS_C = psm("PS_C", [128, 1024])
    PS_D = psm("PS_D", [128, 1024])

    _barctr = {"Vf": [0], "Sf": [0]}

    def _mkbar(semn):
        ctr = _barctr[semn]

        def bar(e, ins):
            ctr[0] += 1
            ins.then_inc(p.sems[semn], 1)
            e.wait_ge(p.sems[semn], ctr[0])
        return bar
    barV = _mkbar("Vf")
    barS = _mkbar("Sf")

    def fn_consts(e):
        return [e.dma_start(out=cs[k][:], in_=const_t[k].ap())
                for k in const_t if k != "id32"]
    tok_consts = p.op_dmas_n(SY, fn_consts, len(const_t) - 1, inc="dm0")

    def fn_vinit(e):
        e.memset(cpool[:, 0:1], -6.0)
        e.memset(PS_A[:], 0.0)
        e.memset(PS_B[:], 0.0)
        e.memset(PS_C[:], 0.0)
        e.memset(PS_D[:], 0.0)
        return e.memset(bass.AP(vt, 64, [[NT * 520, 128], [520, NT], [65, 8],
                                         [1, 1]]), 1.0)
    tok_vinit = p.op(V, fn_vinit, inc="V")

    def fn_wv(e):
        return e.dma_start(
            out=bass.AP(wv_sb, 0, [[NK * 512, 128], [512, NK], [1, 512]]),
            in_=bass.AP(wv_in.tensor, 0, [[512, 128], [512 * 128, NK],
                                          [1, 512]]))
    tok_wv = p.op(G, fn_wv, inc="dq0", inc_n=16)

    def load_wq(src_ap, col0, par, waits):
        w_ = src_ap.tensor.shape[1]

        def fn(e, col0=col0, w_=w_):
            return e.dma_start(
                out=bass.AP(wq_sb, 0, [[D, 128], [128, NK], [1, 128]]),
                in_=bass.AP(src_ap.tensor, col0,
                            [[w_, 128], [128 * w_, NK], [1, 128]]))
        return p.op(G, fn, waits=waits, inc="dq0" if par == 0 else "dq1",
                    inc_n=16)

    def xsl3(t):
        sl = t % 3
        if sl < 2:
            return xin, sl * D
        return xin3, 0

    def load_x(t, waits, oddq=None):
        tens, c0 = xsl3(t)

        def fn(e, t=t, tens=tens, c0=c0):
            return e.dma_start(out=tens[:, c0:c0 + D],
                               in_=x_in[128 * t:128 * (t + 1), :])
        if t % 2 == 0:
            return p.op(SY, fn, waits=waits, inc="dx0", inc_n=16)
        if oddq is None:
            return p.op(SC, fn, waits=waits, inc="dx1", inc_n=16)
        return p.op(oddq, fn, waits=waits, inc="dx2", inc_n=16)

    tok_x = {0: load_x(0, ()), 1: load_x(1, ()), 2: load_x(2, ())}
    tok_hT_ev = {}
    tok_tr_prev = {0: None, 1: None}
    tok_straw_free = {0: None, 1: None}
    tok_stA = {}
    for t in range(NT):
        tens, c0 = xsl3(t)

        def fn_stats(e, t=t, tens=tens, c0=c0):
            for c_ in range(4):
                ins = e.bn_stats(bn6[:, 6 * c_:6 * c_ + 6],
                                 tens[:, c0 + 512 * c_:c0 + 512 * (c_ + 1)])
            barV(e, ins)
            barV(e, e.bn_aggr(stats[:, 48:50], bn6[:, 0:24]))
            e.tensor_copy(stats[:, t:t + 1], stats[:, 48:49])
            return e.tensor_copy(stats[:, 16 + t:17 + t], stats[:, 49:50])
        tok_stA[t] = p.op(V, fn_stats, waits=(tok_x[t], tok_vinit), inc="V")
        if t + 3 < NT:
            tok_x[t + 3] = load_x(t + 3, (tok_stA[t],))

    def fn_batch(e):
        barV(e, e.tensor_scalar(stats[:, 48:64], stats[:, 16:32], EPS, None,
                                ALU.add))
        return e.reciprocal(stats[:, 16:32], stats[:, 48:64])
    tok_bA = p.op(V, fn_batch, waits=(tok_stA[NT - 1],), inc="V")
    tok_bS = p.op(SC, lambda e: e.activation(
        stats[:, 16:32], stats[:, 16:32], AF.Sqrt, bias=0.0, scale=1.0),
        waits=(tok_bA,), inc="Sc")

    def fn_newt(e):
        for _ in range(2):
            barV(e, e.tensor_mul(nsc[:, 0:16], stats[:, 16:32],
                                 stats[:, 16:32]))
            barV(e, e.tensor_mul(nsc[:, 0:16], nsc[:, 0:16],
                                 stats[:, 48:64]))
            barV(e, e.tensor_scalar(nsc[:, 0:16], nsc[:, 0:16], -0.5, 1.5,
                                    ALU.mult, ALU.add))
            barV(e, e.tensor_mul(stats[:, 16:32], stats[:, 16:32],
                                 nsc[:, 0:16]))
        barV(e, e.tensor_mul(stats[:, 32:48], stats[:, 0:16],
                             stats[:, 16:32]))
        return e.tensor_scalar(stats[:, 32:48], stats[:, 32:48], -1.0, None,
                               ALU.mult)
    tok_ln1 = p.op(V, fn_newt, waits=(tok_bS,), inc="V")

    tok_x = {0: load_x(0, (tok_ln1,)), 1: load_x(1, (tok_ln1,)),
             2: load_x(2, (tok_ln1,))}
    for t in range(NT):
        tens, c0 = xsl3(t)
        spar2 = t % 2
        strawX = straw if spar2 == 0 else kbuf
        waits = [tok_x[t], tok_ln1]
        if tok_straw_free[spar2] is not None:
            waits.append(tok_straw_free[spar2])

        def fn_h(e, t=t, tens=tens, c0=c0, strawX=strawX):
            return e.activation(strawX[:, 0:D], tens[:, c0:c0 + D],
                                AF.Identity, bias=stats[:, 32 + t:33 + t],
                                scale=stats[:, 16 + t:17 + t])
        tok_h = p.op(SC, fn_h, waits=waits, inc="Sc")
        tok_st = tok_ln1

        if t + 3 < NT:
            tok_x[t + 3] = load_x(t + 3, (tok_h,))

        for half, psu in ((0, PS_A), (1, PS_B)):
            waits = [tok_h, tok_consts, tok_vinit]
            if tok_tr_prev[half] is not None:
                waits.append(tok_tr_prev[half])

            def fn_tr(e, half=half, psu=psu, strawX=strawX):
                for j8 in range(8):
                    j = half * 8 + j8
                    ins = e.matmul(psu[:, 128 * j8:128 * (j8 + 1)],
                                   strawX[:, 128 * j:128 * (j + 1)],
                                   cs["identB"][:], start=True, stop=True)
                return ins
            tok_tr = p.op(T, fn_tr, waits=waits, inc="T")

            def fn_ev(e, t=t, half=half, psu=psu):
                return e.tensor_copy(
                    bass.AP(hT, S * (half * 8) + 128 * t,
                            [[NK * S, 128], [S, 8], [1, 128]]),
                    psu[:, 0:1024])
            tok_hT_ev[(t, half)] = p.op(V, fn_ev, waits=(tok_tr,), inc="V")
            tok_tr_prev[half] = tok_hT_ev[(t, half)]
        tok_straw_free[spar2] = tok_tr

    tok_hT_done = (tok_hT_ev[(NT - 1, 0)], tok_hT_ev[(NT - 1, 1)])

    from collections import deque
    prodq = deque()
    epiq = deque()

    def drain(q, n):
        for _ in range(min(n, len(q))):
            q.popleft()()

    def drain_all(q):
        drain(q, len(q))

    state = {"psA_free": tok_hT_ev[(NT - 1, 0)],
             "psB_free": tok_hT_ev[(NT - 1, 1)],
             "straw_T": {0: tok_straw_free[0], 1: tok_straw_free[0]},
             "st1_init": None,
             }

    def enq_production(j, out, dst, dst_col0, dst_free_fn):
        out["t1"] = {}
        out["evac"] = {}
        for half in range(2):
            def run_acc(half=half, j=j):
                def fn(e, half=half):
                    for m in range(32):
                        nc2, k = m // 16, m % 16
                        ins = e.matmul(
                            PS_A[:, 512 * nc2:512 * (nc2 + 1)],
                            wq_sb[:, 128 * k:128 * (k + 1)],
                            hT[:, S * k + 1024 * half + 512 * nc2:
                               S * k + 1024 * half + 512 * (nc2 + 1)],
                            start=(k == 0), stop=(k == NK - 1))
                    return ins
                waits = [loads_tok[j], state["psA_free"]] + list(tok_hT_done)
                tok = p.op(T, fn, waits=waits, inc="T")
                out["acc"] = tok
                if half == 1 and j + 1 < len(sched):
                    issue_load(j + 1, (tok,))
            prodq.append(run_acc)

            def run_evac(half=half, out=out):
                def fn(e, half=half):
                    return e.tensor_copy(straw[:, 1024 * half:1024 * (half + 1)],
                                         PS_A[:, 0:1024])
                waits = [out["acc"], state["straw_T"][half]]
                out["evac"][half] = p.op(V, fn, waits=waits, inc="V")
            prodq.append(run_evac)

            def run_swap(half=half, out=out):
                def fn(e, half=half):
                    for nc2 in range(2):
                        ins = e.matmul(PS_A[:, 512 * nc2:512 * (nc2 + 1)],
                                       cs["PERM"][:],
                                       straw[:, 1024 * half + 512 * nc2:
                                             1024 * half + 512 * (nc2 + 1)],
                                       start=True, stop=True)
                    return ins
                tok = p.op(T, fn, waits=[out["evac"][half]], inc="T")
                state["straw_T"][half] = tok
                out["swap"] = tok
            prodq.append(run_swap)

            def run_t1(half=half, out=out):
                def fn(e, half=half):
                    return e.tensor_tensor(
                        st1[:, 1024 * half:1024 * (half + 1)], PS_A[:, 0:1024],
                        cs["S128"][:, 1024 * half:1024 * (half + 1)], ALU.mult)
                waits = [out["swap"]]
                if state["st1_init"] is not None:
                    waits.append(state["st1_init"])
                    state["st1_init"] = None
                tok = p.op(V, fn, waits=waits, inc="V")
                out["t1"][half] = tok
                state["psA_free"] = tok
            prodq.append(run_t1)

            def run_fin(half=half, out=out):
                def fn(e, half=half):
                    barV(e, e.tensor_tensor(
                        st2[:, 0:1024],
                        straw[:, 1024 * half:1024 * (half + 1)],
                        cs["C128"][:, 1024 * half:1024 * (half + 1)],
                        ALU.mult))
                    return e.tensor_add(
                        dst[:, dst_col0 + 1024 * half:
                            dst_col0 + 1024 * (half + 1)],
                        st2[:, 0:1024],
                        st1[:, 1024 * half:1024 * (half + 1)])
                waits = [out["t1"][half]]
                if half == 0:
                    df = dst_free_fn()
                    if df is not None:
                        waits.append(df)
                out["ready"] = p.op(V, fn, waits=waits, inc="V")
            prodq.append(run_fin)

    sched = []
    for kt_ in range(4):
        sched.append((wk_in, 128 * kt_))
        for sp_ in range(4 * kt_, 4 * kt_ + 4):
            sched.append((wq_in, 128 * sp_))
    loads_tok = {}

    def issue_load(j, waits):
        src, col = sched[j]
        loads_tok[j] = load_wq(src, col, j % 2, waits)

    issue_load(0, (tok_wv,))
    kprod = [{} for _ in range(4)]
    qprod = [{} for _ in range(NSTRIP)]
    enq_production(0, kprod[0], kbuf, 0, lambda: tok_straw_free[1])
    enq_production(1, qprod[0], strip, 0, lambda: None)

    tok_v_prev = {0: None, 1: None}
    for t in range(NT):
        hp = t % 2

        def fn_vmm(e, t=t, hp=hp):
            for k in range(NK):
                ins = e.matmul(PS_C[:, 512 * hp:512 * (hp + 1)],
                               hT[:, S * k + 128 * t:S * k + 128 * (t + 1)],
                               wv_sb[:, 512 * k:512 * (k + 1)],
                               start=(k == 0), stop=(k == NK - 1))
            return ins
        waits = [tok_wv] + list(tok_hT_done)
        if tok_v_prev[hp] is not None:
            waits.append(tok_v_prev[hp])
        tok_vmm = p.op(T, fn_vmm, waits=waits, inc="T")

        def fn_vev(e, t=t, hp=hp):
            return e.tensor_copy(
                bass.AP(vt, 520 * t, [[NT * 520, 128], [65, 8], [1, 64]]),
                bass.AP(PS_C, 512 * hp, [[1024, 128], [64, 8], [1, 64]]))
        tok_v_prev[hp] = p.op(V, fn_vev, waits=(tok_vmm, tok_vinit), inc="V")
        tok_wv_free = tok_vmm
        drain(prodq, 2)
    tok_v_done = tok_v_prev[(NT - 1) % 2]
    tok_v_done2 = tok_v_prev[NT % 2]


    def load_wf(t, waits, eng=None):
        def fn(e, t=t):
            return e.dma_start(out=wf_sb[:, (t % 2) * D:(t % 2 + 1) * D],
                               in_=wt_in[128 * t:128 * (t + 1), :])
        return p.op(eng or SY, fn, waits=waits,
                    inc="dwf0" if t % 2 == 0 else "dwf1", inc_n=16)

    tok_wf = [load_wf(0, (tok_wv_free,)), load_wf(1, (tok_wv_free,))]
    tok_wred = None
    for t in range(NT):
        par = t % 2

        def fn_wred(e, t=t, par=par):
            ins = e.reduce_max(nsc[:, 48 + t:49 + t],
                               wf_sb[:, par * D:(par + 1) * D], AX.X,
                               apply_absolute_value=True)
            if t == NT - 1:
                barV(e, ins)
                ins = e.engine_nop()
            return ins
        tok_wred = p.op(V, fn_wred, waits=(tok_wf[par],), inc="V")
        if t + 2 < NT:
            tok_wf[par] = load_wf(t + 2, (tok_wred,))

    def fn_wamax(e):
        return e.reduce_max(scl[:, 14:15], nsc[:, 48:64], AX.X)
    tok_wa_v = p.op(V, fn_wamax, waits=(tok_wred,), inc="V")

    tok_wrow = p.op(SY, lambda e: e.dma_start(out=wrowW[:],
                                              in_=scl[:, 14:15]),
                    waits=(tok_wa_v, tok_h), inc="dm1", inc_n=16)

    def fn_wtree(e):
        barV(e, e.reduce_max(scl[0:1, 15:16], wrowW[0:1, :], AX.X))
        barV(e, e.tensor_scalar(scl[0:1, 10:11], scl[0:1, 15:16],
                                float(np.float32(1.0) / np.float32(448.0)),
                                1e-12, ALU.mult, ALU.max))
        barV(e, e.tensor_scalar(sscr[0:1, 0:1], scl[0:1, 10:11], 2.0, None,
                                ALU.mult))
        return e.reciprocal(sscr[0:1, 1:2], sscr[0:1, 0:1])
    tok_wsc = p.op(V, fn_wtree, waits=(tok_wrow,), inc="V")
    tok_qsw = p.op(SY, lambda e: e.dma_start(
        out=scl[:, 12:13], in_=bass.AP(sscr, 1, [[8, 1], [0, 128]])),
        waits=(tok_wsc,), inc="dm0", inc_n=16)
    tok_wf2 = [load_wf(0, (tok_wa_v,), eng=SC), load_wf(1, (tok_wa_v,),
                                                        eng=SC)]

    if stop_after in ("ln1", "kv"):
        _finish(nc, p, stack, dbg, locals(), stop_after)
        return nc

    def fn_kdup(e, g):
        lo = (g % 2) * 64
        dpar = (g % 2) * S
        return [
            e.dma_start(out=kdup[0:64, dpar:dpar + S],
                        in_=kbuf[lo:lo + 64, :]),
            e.dma_start(out=kdup[64:128, dpar:dpar + S],
                        in_=kbuf[lo:lo + 64, :])]

    tok_strip = None
    tok_strip_free = None
    tok_kst = None
    tok_kbuf_free = None
    tok_kdup = None
    tok_scp_free = {0: tok_v_done, 1: None}
    tok_pb_free = {0: None, 1: None}
    tok_psd_free = None
    tok_staged_free = None
    tok_dent_prev = None
    tok_stt = None
    state["st1_init"] = tok_h
    tok_xin_V_free = tok_st
    tok_xin_Sc_free = tok_h
    tok_scp_free[1] = tok_hT_ev[(NT - 1, 1)]
    nlu = {"stt": None, "staged_free": None, "dent_prev": None}
    stt_bank = {0: None, 1: None}

    tok_strip_free_par = [None, None]
    state2 = {"kbuf_free": None}
    kdup_last_read = {0: None, 1: None}
    drain_all(prodq)
    tok_kst = kprod[0]["ready"]
    tok_kdup = p.op_dmas_n(SY, lambda e: fn_kdup(e, 0), 2,
                           waits=(tok_kst, tok_wv_free), inc="dm0")
    tok_strip = qprod[0]["ready"]

    for sp in range(NSTRIP):
        g = sp // 2
        spar = sp % 2
        if sp + 1 < NSTRIP:
            jq = (sp + 1) + (sp + 1) // 4 + 1
            npar = (sp + 1) % 2
            enq_production(jq, qprod[sp + 1], strip, npar * S,
                           lambda npar=npar: tok_strip_free_par[npar])
        if sp % 4 == 2 and sp + 2 < NSTRIP:
            kt = (sp + 2) // 4
            enq_production(5 * kt, kprod[kt], kbuf, 0,
                           lambda: state2["kbuf_free"])

        tok_av = {}
        tok_pvev = None
        last_T_tok = None
        for qc in range(4):
            nsi = 4 * qc + 4
            tok_sc = {}
            for si in range(nsi + 1):
                if qc == 0:
                    drain(epiq, 8 if si == 0 else 2)
                else:
                    drain(epiq, 1)
                drain(prodq, 2 if len(prodq) > 20 else 1)
                if si < nsi:
                    i = si - 4 * qc
                    cst = 128 * i if i > 0 else 0
                    pp = si % 2
                    scp = PS_C if pp == 0 else PS_B

                    def fn_sc(e, si=si, qc=qc, cst=cst, scp=scp,
                              spar=spar, gpar=g % 2, part=(i >= 0)):
                        for base in (0, 64):
                            hh = base // 64
                            ins = e.matmul(
                                scp[:, 512 * hh + cst:512 * (hh + 1)],
                                kdup[base:base + 64,
                                     gpar * S + 128 * si:
                                     gpar * S + 128 * (si + 1)],
                                strip[base:base + 64,
                                      spar * S + 512 * qc + cst:
                                      spar * S + 512 * (qc + 1)],
                                start=True, stop=not part)
                            if part:
                                ins = e.matmul(
                                    scp[:, 512 * hh + cst:
                                        512 * hh + cst + 128],
                                    cs["UM2"][:], cs["identB"][:],
                                    start=False, stop=True)
                        return ins
                    waits = [tok_strip, tok_kdup]
                    if tok_scp_free[pp] is not None:
                        waits.append(tok_scp_free[pp])
                    if qc == 0 and si == 0 and nlu["dent_prev"] is not None:
                        waits.append(nlu["dent_prev"])
                    tok_sc[si] = p.op(T, fn_sc, waits=waits, inc="T")
                    last_T_tok = tok_sc[si]

                    def fn_exp(e, scp=scp, pp=pp):
                        return e.activation(pbuf[:, 1024 * pp:1024 * (pp + 1)],
                                            scp[:, 0:1024], AF.Exp,
                                            bias=cpool[:, 0:1], scale=0.125)
                    waits = [tok_sc[si], tok_vinit, tok_xin_V_free,
                             tok_xin_Sc_free]
                    if tok_pb_free[pp] is not None:
                        waits.append(tok_pb_free[pp])
                    tok_e = p.op(SC, fn_exp, waits=waits, inc="Sc")
                    tok_scp_free[pp] = tok_e

                    tok_sc[si] = (tok_sc[si], (tok_e, None))

                if si > 0:
                    sj = si - 1
                    i = sj - 4 * qc
                    cst = 128 * i if i > 0 else 0
                    pp = sj % 2
                    (_, (tok_e, tok_m)) = tok_sc[sj]

                    def fn_av(e, sj=sj, cst=cst, pp=pp, g=g, nsi=nsi):
                        for hh in range(2):
                            ins = e.matmul(
                                PS_D[0:65, 512 * hh + cst:512 * (hh + 1)],
                                vt[:, 520 * sj + 65 * g:520 * sj + 65 * (g + 1)],
                                pbuf[:, 1024 * pp + 512 * hh + cst:
                                     1024 * pp + 512 * (hh + 1)],
                                start=(sj == 0), stop=(sj == nsi - 1))
                        return ins
                    waits = [tok_e, tok_v_done, tok_v_done2]
                    if tok_m is not None:
                        waits.append(tok_m)
                    if sj == 0 and tok_psd_free is not None:
                        waits.append(tok_psd_free)
                    tok_a = p.op(T, fn_av, waits=waits, inc="T")
                    tok_pb_free[pp] = tok_a
                    last_T_tok = tok_a
                    tok_av[(qc, sj)] = tok_a

            waits = [tok_av[(qc, nsi - 1)]]
            if qc == 0:
                waits += [w for w in (nlu["staged_free"], nlu["dent_prev"],
                                      tok_xin_V_free, tok_xin_Sc_free)
                          if w is not None]

            def fn_pvev(e, qc=qc):
                e.tensor_copy(stagedA[0:65, 512 * qc:512 * (qc + 1)],
                              PS_D[0:65, 0:512])
                return e.tensor_copy(stagedB[0:65, 512 * qc:512 * (qc + 1)],
                                     PS_D[0:65, 512:1024])
            tok_pvev = p.op(V, fn_pvev, waits=waits, inc="V")
            tok_psd_free = tok_pvev

        def mk_epilogue(sp=sp, g=g, spar=spar, tok_pvev=tok_pvev):
            eps = {}

            def run_dtr():
                def fn(e):
                    for hh, stg in ((0, stagedA), (1, stagedB)):
                        for j in range(16):
                            ins = e.matmul(
                                PS_C[:, 512 + 16 * hh + j:
                                     512 + 16 * hh + j + 1],
                                stg[64:65, 128 * j:128 * (j + 1)],
                                cs["identB"][64:65, 64:65],
                                start=True, stop=True)
                    return ins
                waits = [tok_pvev]
                if tok_scp_free[0] is not None:
                    waits.append(tok_scp_free[0])
                eps["dtr"] = p.op(T, fn, waits=waits, inc="T")
            epiq.append(run_dtr)

            def run_rec():
                def fn(e):
                    return e.reciprocal(nsc[:, 0:32], PS_C[:, 512:544])
                eps["rec"] = p.op(V, fn, waits=(eps["dtr"],), inc="V")
                nlu["dent_prev"] = eps["rec"]
            epiq.append(run_rec)

            headA = 4 * g + 2 * spar

            def mk_ctr(a, hj, need_rec=False):
                psu = PS_C if hj == 0 else PS_B
                stg = stagedA if a == 0 else stagedB
                pp = hj

                def run_ctr():
                    def fn(e, a=a, hj=hj, psu=psu, stg=stg):
                        for j8 in range(8):
                            j = hj * 8 + j8
                            ins = e.matmul(
                                psu[:, 512 * a + 64 * j8:
                                    512 * a + 64 * (j8 + 1)],
                                stg[0:64, 128 * j:128 * (j + 1)],
                                cs["identB"][0:64, 0:64],
                                start=True, stop=True)
                        return ins
                    waits = [tok_pvev]
                    if tok_scp_free[pp] is not None:
                        waits.append(tok_scp_free[pp])
                    if need_rec:
                        waits.append(eps["rec"])
                    eps[("ctr", a, hj)] = p.op(T, fn, waits=waits, inc="T")
                    nlu["staged_free"] = eps[("ctr", a, hj)]
                return run_ctr

            def mk_stt(a, hj):
                psu = PS_C if hj == 0 else PS_B
                head = headA + a
                pp = hj

                def run_stt():
                    def fn(e, a=a, hj=hj, psu=psu, head=head):
                        for j8 in range(8):
                            j = hj * 8 + j8
                            xsl = x_bf[:, D * j + 64 * head:
                                       D * j + 64 * (head + 1)]
                            ins = e.tensor_scalar(
                                xsl,
                                psu[:, 512 * a + 64 * j8:
                                    512 * a + 64 * (j8 + 1)],
                                nsc[:, 16 * a + j:16 * a + j + 1],
                                None, ALU.mult)
                        return ins
                    tok = p.op(V, fn, waits=(eps[("ctr", a, hj)],
                                             eps["rec"]), inc="V")
                    nlu["stt"] = tok
                    tok_scp_free[pp] = tok
                return run_stt

            epiq.append(mk_ctr(0, 0))
            epiq.append(mk_ctr(0, 1))
            epiq.append(mk_ctr(1, 0, need_rec=True))
            epiq.append(mk_ctr(1, 1))
            epiq.append(mk_stt(0, 0))
            epiq.append(mk_stt(1, 0))
            epiq.append(mk_stt(0, 1))
            epiq.append(mk_stt(1, 1))
        mk_epilogue()

        tok_strip_free = last_T_tok
        tok_strip_free_par[spar] = last_T_tok
        drain_all(prodq)
        if sp + 1 < NSTRIP:
            ng = (sp + 1) // 2
            if (sp + 1) % 4 == 0:
                tok_kst = kprod[(sp + 1) // 4]["ready"]
            if (sp + 1) % 2 == 0:
                tok_kdup = p.op_dmas_n(
                    SY, lambda e, ng=ng: fn_kdup(e, ng), 2,
                    waits=(tok_kst, kdup_last_read.get(ng % 2)), inc="dm0")
                if ng % 2 == 1:
                    state2["kbuf_free"] = tok_kdup
            tok_strip = qprod[sp + 1]["ready"]
        if spar == 1:
            kdup_last_read[g % 2] = last_T_tok
    drain_all(epiq)
    drain_all(prodq)
    tok_stt = nlu["stt"]
    tok_staged_free = nlu["staged_free"]

    if stop_after in ("strip0", "attn"):
        _finish(nc, p, stack, dbg, locals(), stop_after)
        return nc

    tok_x = {0: load_x(0, (tok_stt, tok_staged_free, tok_strip_free)),
             1: load_x(1, (tok_stt, tok_staged_free, tok_strip_free),
                       oddq=G),
             2: load_x(2, (tok_stt, tok_staged_free, tok_strip_free))}
    tok_s2 = {}
    for t in range(NT):
        tens, c0 = xsl3(t)

        def fn_x2(e, t=t, tens=tens, c0=c0):
            xsl = tens[:, c0:c0 + D]
            return e.tensor_add(xsl, xsl, x_bf[:, t * D:(t + 1) * D])
        tok_x2 = p.op(V, fn_x2, waits=(tok_x[t], tok_stt), inc="V")

        def fn_s2(e, t=t, tens=tens, c0=c0):
            xv_ = tens[:, c0:c0 + D]
            for c_ in range(4):
                ins = e.bn_stats(bn6[:, 6 * c_:6 * c_ + 6],
                                 tens[:, c0 + 512 * c_:c0 + 512 * (c_ + 1)])
            barV(e, ins)
            barV(e, e.bn_aggr(stats[:, 48:50], bn6[:, 0:24]))
            e.tensor_copy(stats[:, t:t + 1], stats[:, 48:49])
            e.tensor_copy(stats[:, 16 + t:17 + t], stats[:, 49:50])
            e.reduce_max(nsc[:, 16 + t:17 + t], xv_, AX.X)
            return e.tensor_reduce(nsc[:, 32 + t:33 + t], xv_, AX.X, ALU.min)
        tok_s2[t] = p.op(V, fn_s2, waits=(tok_x2,), inc="V")
        if t + 3 < NT:
            tok_x[t + 3] = load_x(t + 3, (tok_s2[t],), oddq=G)

    tok_w8 = None
    for t in range(NT):
        par = t % 2

        def fn_w8(e, t=t, par=par):
            return e.activation(w8[:, t * D:(t + 1) * D],
                                wf_sb[:, par * D:(par + 1) * D], AF.Copy,
                                bias=0.0, scale=scl[:, 12:13])
        tok_w8 = p.op(SC, fn_w8, waits=(tok_wf2[par], tok_qsw,
                                        tok_strip_free), inc="Sc")
        if t + 2 < NT:
            tok_wf2[par] = load_wf(t + 2, (tok_w8,), eng=SC)
    tok_w8_done = tok_w8

    def fn_b2(e):
        barV(e, e.tensor_scalar(stats[:, 48:64], stats[:, 16:32], EPS, None,
                                ALU.add))
        return e.reciprocal(stats[:, 16:32], stats[:, 48:64])
    tok_b2 = p.op(V, fn_b2, waits=(tok_s2[NT - 1],), inc="V")
    tok_b2s = p.op(SC, lambda e: e.activation(
        stats[:, 16:32], stats[:, 16:32], AF.Sqrt, bias=0.0, scale=1.0),
        waits=(tok_b2,), inc="Sc")

    def fn_n2(e):
        for _ in range(2):
            barV(e, e.tensor_mul(nsc[:, 0:16], stats[:, 16:32],
                                 stats[:, 16:32]))
            barV(e, e.tensor_mul(nsc[:, 0:16], nsc[:, 0:16],
                                 stats[:, 48:64]))
            barV(e, e.tensor_scalar(nsc[:, 0:16], nsc[:, 0:16], -0.5, 1.5,
                                    ALU.mult, ALU.add))
            barV(e, e.tensor_mul(stats[:, 16:32], stats[:, 16:32],
                                 nsc[:, 0:16]))
        barV(e, e.tensor_mul(stats[:, 32:48], stats[:, 0:16],
                             stats[:, 16:32]))
        barV(e, e.tensor_scalar(stats[:, 32:48], stats[:, 32:48], -1.0, None,
                                ALU.mult))
        barV(e, e.tensor_sub(nsc[:, 16:32], nsc[:, 16:32], stats[:, 0:16]))
        barV(e, e.tensor_sub(nsc[:, 32:48], stats[:, 0:16], nsc[:, 32:48]))
        barV(e, e.tensor_max(nsc[:, 16:32], nsc[:, 16:32], nsc[:, 32:48]))
        barV(e, e.tensor_mul(nsc[:, 0:16], nsc[:, 16:32], stats[:, 16:32]))
        return e.engine_nop()
    tok_ln2 = p.op(V, fn_n2, waits=(tok_b2s,), inc="V")

    tok_am_v = p.op(V, lambda e: e.reduce_max(nsc[:, 16:17], nsc[:, 0:16],
                                              AX.X), waits=(tok_ln2,),
                    inc="V")

    tok_arow = p.op(SY, lambda e: e.dma_start(out=wrowA[:],
                                              in_=nsc[:, 16:17]),
                    waits=(tok_am_v,), inc="dm1", inc_n=16)

    def fn_amr(e):
        barV(e, e.reduce_max(nsc[0:1, 17:18], wrowA[0:1, :], AX.X))
        return e.tensor_copy(scl[0:1, 0:8],
                             bass.AP(nsc, 17, [[64, 1], [0, 8]]))
    tok_amr = p.op(V, fn_amr, waits=(tok_arow,), inc="V")

    tok_cc1 = p.op(G, lambda e: e.dma_start(out=cc_in.ap(),
                                            in_=scl[0:1, 0:8]),
                   waits=(tok_amr,), inc="dq0", inc_n=16)
    tok_cc2 = p.op(G, lambda e: e.collective_compute(
        "AllReduce", ALU.max, replica_groups=[list(range(cc_cores))],
        ins=[cc_in.ap()], outs=[cc_out.ap()]), waits=(tok_cc1,), inc="dcc")
    tok_cc3 = p.op(G, lambda e: e.dma_start(out=scl[0:1, 8:9],
                                            in_=cc_out.ap()[0:1, 0:1]),
                   waits=(tok_cc2,), inc="dq1", inc_n=16)

    def fn_xscale(e):
        barV(e, e.tensor_scalar(scl[0:1, 9:10], scl[0:1, 8:9],
                                float(np.float32(1.0) / np.float32(448.0)),
                                1e-12, ALU.mult, ALU.max))
        barV(e, e.tensor_scalar(sscr[0:1, 2:3], scl[0:1, 9:10], 2.0, None,
                                ALU.mult))
        e.reciprocal(sscr[0:1, 3:4], sscr[0:1, 2:3])
        barV(e, e.tensor_mul(sscr[0:1, 4:5], scl[0:1, 9:10],
                             scl[0:1, 10:11]))
        barV(e, e.tensor_scalar(sscr[0:1, 5:6], sscr[0:1, 4:5], 4.0, None,
                                ALU.mult))
        return e.engine_nop()
    tok_xsc = p.op(V, fn_xscale, waits=(tok_cc3, tok_wsc), inc="V")

    tok_qsx = p.op(SY, lambda e: e.dma_start(
        out=scl[:, 11:12], in_=bass.AP(sscr, 3, [[8, 1], [0, 128]])),
        waits=(tok_xsc,), inc="dm0", inc_n=16)
    tok_c4 = p.op(SY, lambda e: e.dma_start(
        out=scl[:, 13:14], in_=bass.AP(sscr, 5, [[8, 1], [0, 128]])),
        waits=(tok_xsc,), inc="dm1", inc_n=16)

    def fn_ab(e):
        e.tensor_tensor(ab[:, 0:16], stats[:, 16:32],
                        bass.AP(scl, 11, [[16, 128], [0, 16]]), ALU.mult)
        return e.tensor_tensor(ab[:, 16:32], stats[:, 32:48],
                               bass.AP(scl, 11, [[16, 128], [0, 16]]),
                               ALU.mult)
    tok_ab = p.op(V, fn_ab, waits=(tok_qsx, tok_ln2), inc="V")

    def fn_id8(e):
        return e.activation(id8[:], cs["identB"][:], AF.Copy, bias=0.0,
                            scale=1.0)
    tok_id8 = p.op(SC, fn_id8, waits=(tok_consts, tok_strip_free), inc="Sc")

    xslots = [(xin, 0), (xin, D), (xin3, 0), (xin4, 0)]

    def xsl_t(t):
        return xslots[t % 4]

    def load_x5(t, waits):
        tens, c0 = xsl_t(t)

        def fn(e, t=t, tens=tens, c0=c0):
            return e.dma_start(out=tens[:, c0:c0 + D],
                               in_=x_in[128 * t:128 * (t + 1), :])
        return p.op(G, fn, waits=tuple(w for w in waits if w is not None),
                    inc="dq0" if t % 2 == 0 else "dq1", inc_n=16)

    psbanks = [(PS_C, 0), (PS_C, 512), (PS_D, 0), (PS_D, 512)]
    tok_h2ev = {}
    tok_h2tr = {0: None, 1: None}
    tok_q = [None, None]
    tok_qd = {}
    tok_x5 = {}
    tok_fev = {}
    tok_addv = {0: None, 1: None}
    tok_outd = {}

    tok_x5[0] = load_x5(0, (tok_ln2,))
    tok_x5[1] = load_x5(1, (tok_ln2,))
    tok_x5[2] = load_x5(2, (tok_ln2,))
    tok_x5[3] = load_x5(3, (tok_strip_free, kdup_last_read.get(0),
                            kdup_last_read.get(1)))

    def emit_qx(t):
        par = t % 2
        tens, c0 = xsl_t(t)

        def fn_x2(e, t=t, tens=tens, c0=c0):
            xsl = tens[:, c0:c0 + D]
            return e.tensor_add(xsl, xsl, x_bf[:, t * D:(t + 1) * D])
        tok_x2q = p.op(V, fn_x2, waits=(tok_x5[t],), inc="V")

        def fn_q(e, t=t, par=par, tens=tens, c0=c0):
            return e.activation(h2qn[:, par * D:(par + 1) * D],
                                tens[:, c0:c0 + D], AF.Identity,
                                bias=ab[:, 16 + t:17 + t],
                                scale=ab[:, t:t + 1])
        waits = [tok_ab, tok_x2q]
        if t < 2:
            waits.append(tok_strip_free)
        if tok_h2tr[par] is not None:
            waits.append(tok_h2tr[par])
        tok_q[par] = tok_qd[t] = p.op(SC, fn_q, waits=waits, inc="Sc")

    def emit_tr(t):
        par = t % 2
        tok_h2ev[t] = {}
        for half, psu in ((0, PS_A), (1, PS_B)):
            def fn_qtr(e, par=par, half=half, psu=psu):
                for j8 in range(8):
                    j = half * 8 + j8
                    ins = e.matmul(
                        psu[:, 128 * j8:128 * (j8 + 1)],
                        h2qn[:, par * D + 128 * j:par * D + 128 * (j + 1)],
                        id8[:], start=True, stop=True)
                return ins
            waits = [tok_q[par], tok_id8]
            if t >= 1:
                waits += list(tok_h2ev[t - 1][half])
            if t == 0 and half == 0:
                waits.append(state["psA_free"])
                waits.append(tok_scp_free[1])
            tok_qt = p.op(T, fn_qtr, waits=waits, inc="T")
            if half == 1:
                tok_h2tr[par] = tok_qt

            if half == 0:
                def fn_evA(e, t=t, psu=psu):
                    return e.tensor_copy(
                        bass.AP(h2q, 128 * t,
                                [[NK * S, 128], [S, 8], [1, 128]]),
                        psu[:, 0:1024])
                tok_h2ev[t][0] = (p.op(V, fn_evA, waits=(tok_qt,), inc="V"),)
            else:
                def fn_evB1(e, t=t, psu=psu):
                    return e.tensor_copy(
                        bass.AP(h2q, S * 8 + 128 * t,
                                [[NK * S, 128], [S, 3], [1, 128]]),
                        psu[:, 0:384])
                evB1 = p.op(V, fn_evB1, waits=(tok_qt,), inc="V")

                def fn_evB2(e, t=t, psu=psu):
                    return e.activation(
                        bass.AP(h2q, S * 11 + 128 * t,
                                [[NK * S, 128], [S, 5], [1, 128]]),
                        psu[:, 384:1024], AF.Copy, bias=0.0, scale=1.0)
                evB2 = p.op(SC, fn_evB2, waits=(tok_qt,), inc="Sc")
                tok_h2ev[t][1] = (evB1, evB2)

    def emit_ffn(t):
        par = t % 2
        tens, c0 = xsl_t(t)
        for oc in range(4):
            psu, off = psbanks[oc]

            def fn_fmm(e, t=t, oc=oc, psu=psu, off=off):
                for kk in range(NK // 2):
                    ins = e.matmul(
                        psu[:, off:off + 512],
                        bass.AP(h2q, S * 2 * kk + 128 * t,
                                [[NK * S, 128], [S, 2], [1, 128]]),
                        bass.AP(w8, D * 2 * kk + 512 * oc,
                                [[NK * D, 128], [D, 2], [1, 512]]),
                        start=(kk == 0), stop=(kk == NK // 2 - 1),
                        perf_mode=mybir.MatmulPerfMode.DoubleRow)
                return ins
            waits = [tok_h2ev[t][0][0], tok_h2ev[t][1][0], tok_h2ev[t][1][1],
                     tok_w8_done]
            if t == 0:
                waits += [tok_scp_free[0], tok_psd_free]
            if tok_fev.get((t - 1, oc)) is not None:
                waits.append(tok_fev[(t - 1, oc)])
            tok_fmm = p.op(T, fn_fmm, waits=waits, inc="T")

            def fn_fev(e, t=t, oc=oc, par=par, psu=psu, off=off):
                return e.activation(
                    ffnb[:, par * D + 512 * oc:par * D + 512 * (oc + 1)],
                    psu[:, off:off + 512], AF.Copy, bias=0.0,
                    scale=scl[:, 13:14])
            waits = [tok_fmm, tok_c4]
            if tok_addv[par] is not None and oc == 0:
                waits.append(tok_addv[par])
            tok_fev[(t, oc)] = p.op(SC, fn_fev, waits=waits, inc="Sc")

        def fn_add(e, t=t, par=par, tens=tens, c0=c0):
            xsl = tens[:, c0:c0 + D]
            return e.tensor_add(xsl, xsl, ffnb[:, par * D:(par + 1) * D])
        waits = [tok_fev[(t, oc_)] for oc_ in range(4)]
        tok_addv[par] = p.op(V, fn_add, waits=waits, inc="V")

        def fn_out(e, t=t, tens=tens, c0=c0):
            return e.dma_start(out=y_out[128 * t:128 * (t + 1), :],
                               in_=tens[:, c0:c0 + D])
        tok_outd[t] = p.op(SY, fn_out, waits=(tok_addv[par],), inc="dout",
                           inc_n=16)
        if t + 4 < NT:
            tok_x5[t + 4] = load_x5(t + 4, (tok_outd[t], tok_qd[t]))

    emit_qx(0)
    for t in range(NT):
        if t + 1 < NT:
            emit_qx(t + 1)
        emit_tr(t)
        if t >= 1:
            emit_ffn(t - 1)
    emit_ffn(NT - 1)

    p.wait(SY, ("dout", 16 * NT))

    if stop_after == "quant":
        _finish(nc, p, stack, dbg, locals(), stop_after)
        return nc

    _finish(nc, p, stack, dbg, locals(), None)
    return nc


_STAGE_DBG = {
    "ln1": ("d_hT", "d_x2", "d_stats"),
    "kv": ("d_hT", "d_vt", "d_x2", "d_stats", "d_nsc", "d_scl"),
    "attn": ("d_hT", "d_vt", "d_x2", "d_stats", "d_nsc", "d_scl", "d_kc",
             "d_strip", "d_staged"),
    "quant": ("d_hT", "d_vt", "d_x2", "d_stats", "d_nsc", "d_scl", "d_kc",
              "d_strip", "d_staged", "d_h2q", "d_w8", "d_ab"),
    None: ("d_hT", "d_vt", "d_x2", "d_stats", "d_nsc", "d_scl", "d_kc",
           "d_strip", "d_staged", "d_h2q", "d_w8", "d_ab"),
}


def _finish(nc, p, stack, dbg, env, stop_after):
    env["commit_arena"]()
    dbg = {k: v for k, v in dbg.items() if k in _STAGE_DBG[stop_after]}
    if dbg:
        srcmap = {"d_hT": "hT", "d_kc": "kbuf", "d_vt": "vt", "d_strip": "strip",
                  "d_x2": "x_bf", "d_stats": "stats", "d_nsc": "nsc",
                  "d_scl": "scl", "d_h2q": "h2q", "d_w8": "w8",
                  "d_ab": None, "d_staged": None}
        for eng in (SY,):
            for semn in ("T", "V", "Sc", "G"):
                p.wait(eng, (semn, p.ctr[semn]))

        def fn_dbg(e):
            out = []
            for name, ap_ in dbg.items():
                if name == "d_staged":
                    out.append(e.dma_start(out=ap_[0:65, 0:S],
                                           in_=env["stagedA"][:]))
                    out.append(e.dma_start(out=ap_[0:65, S:2 * S],
                                           in_=env["stagedB"][:]))
                elif name == "d_ab":
                    out.append(e.dma_start(out=ap_[:, 0:32],
                                           in_=env["ab"][:]))
                else:
                    out.append(e.dma_start(out=ap_, in_=env[srcmap[name]][:]))
            return out
        n_dma = len(dbg) + (1 if "d_staged" in dbg else 0)
        tok_dbg = p.op_dmas_n(SY, fn_dbg, n_dma, inc="dbgS")
        p.wait(SY, tok_dbg)

    with nc.Block() as block, contextlib.ExitStack() as semstack:
        for name in p.ctr:
            p.sems[name] = semstack.enter_context(nc.semaphore(name))
        p.install(block)
    stack.close()


def _prep_inputs(L0_attn_x, L0_attn_Wq, L0_attn_Wk, L0_attn_Wv,
                 L0_ffn_weight):
    qperm = np.concatenate(
        [h * HD + np.concatenate([np.arange(0, HD, 2), np.arange(1, HD, 2)])
         for h in range(32)])
    kperm = qperm[:NKV * HD]
    wq = np.ascontiguousarray(np.asarray(L0_attn_Wq)[:, qperm])
    wk = np.ascontiguousarray(np.asarray(L0_attn_Wk)[:, kperm])
    wv = np.ascontiguousarray(np.asarray(L0_attn_Wv))
    wt = np.ascontiguousarray(np.asarray(L0_ffn_weight).T)
    xs = [np.ascontiguousarray(np.asarray(L0_attn_x)[b])
          for b in range(N_CORES)]
    return xs, wq, wk, wv, wt


def kernel(L0_attn_x, L0_attn_Wq, L0_attn_Wk, L0_attn_Wv, L0_ffn_weight):
    xs, wq, wk, wv, wt = _prep_inputs(L0_attn_x, L0_attn_Wq, L0_attn_Wk,
                                      L0_attn_Wv, L0_ffn_weight)
    nc = build_nc()
    in_maps = [{"x": xs[b], "wq": wq, "wk": wk, "wv": wv, "wt": wt}
               for b in range(N_CORES)]
    res = run_bass_kernel_spmd(nc, in_maps, core_ids=list(range(N_CORES)))
    out = np.stack([np.asarray(res.results[b]["y"]) for b in range(N_CORES)],
                   axis=0)
    return out.astype(np.float32)
